# version 23
# baseline (speedup 1.0000x reference)
import math
import numpy as np
import ml_dtypes
import concourse.bass as bass
import concourse.mybir as mybir
from concourse.bass_utils import run_bass_kernel_spmd

F32 = mybir.dt.float32
BF16 = mybir.dt.bfloat16
AF = mybir.ActivationFunctionType
ALU = mybir.AluOpType
AX = mybir.AxisListType
BF = ml_dtypes.bfloat16

D = 2048
S = 2048
NH = 16
DH = 128
NDC = 16
NIC = 11
NE = 8
TT = 512
HALF = 1024
SCALE = 1.0 / math.sqrt(DH)
EPS = 1e-5

ENGINES = ("te", "act", "dve", "sync", "gp")
ENG_SEM = {"te": "pe", "act": "sa", "dve": "sv", "gp": "po"}

_PHASES = "all"
_FFN_PARTS = ("skip", "router", "shared", "routed", "down")
_ROUTER_EARLY = False


class Buf:
    __slots__ = ("writers", "readers", "name")

    def __init__(self, name=""):
        self.writers = {}
        self.readers = {}
        self.name = name


class KB:
    def __init__(self, nc):
        self.nc = nc
        self.ops = {e: [] for e in ENGINES}
        self.cnt = {}
        self.waited = {e: {} for e in ENGINES}
        self.semcm = []
        self.semh = {}

    def sem(self, name):
        if name not in self.semh:
            cm = self.nc.semaphore(name)
            self.semh[name] = cm.__enter__()
            self.semcm.append(cm)
            self.cnt[name] = 0
        return name

    def _wait(self, eng, sem, val):
        if val <= 0:
            return
        if self.waited[eng].get(sem, 0) >= val:
            return
        self.waited[eng][sem] = val
        self.ops[eng].append(("w", sem, val))

    def emit(self, eng, fns, reads=(), writes=(), dma_key=None):
        if not isinstance(fns, (list, tuple)):
            fns = [fns]
        if dma_key is not None:
            return self._emit_dma(eng, fns, reads, writes, dma_key)
        sem = self.sem(ENG_SEM[eng])
        need = {}
        for b in reads:
            for s, v in b.writers.items():
                need[s] = max(need.get(s, 0), v)
        for b in writes:
            for s, v in b.writers.items():
                need[s] = max(need.get(s, 0), v)
            for s, v in b.readers.items():
                need[s] = max(need.get(s, 0), v)
        for s in sorted(need):
            self._wait(eng, s, need[s])
        if False:
            pass
        else:
            for i, fn in enumerate(fns):
                if i == len(fns) - 1:
                    self.cnt[sem] += 1
                    self.ops[eng].append(("o", fn, sem, 1))
                else:
                    self.ops[eng].append(("o", fn, None, 0))
        v = self.cnt[sem]
        ws = set(id(b) for b in writes)
        for b in writes:
            b.writers[sem] = max(b.writers.get(sem, 0), v)
        for b in reads:
            if id(b) not in ws:
                b.readers[sem] = max(b.readers.get(sem, 0), v)

    def emit_seq(self, eng, steps, writes=()):
        sem = self.sem(ENG_SEM[eng])
        war = {}
        for b in writes:
            for s2, v in b.writers.items():
                war[s2] = max(war.get(s2, 0), v)
            for s2, v in b.readers.items():
                war[s2] = max(war.get(s2, 0), v)
        for s2 in sorted(war):
            self._wait(eng, s2, war[s2])
        nall = len(steps)
        for i, (fn, reads) in enumerate(steps):
            need = {}
            for b in reads:
                for s2, v in b.writers.items():
                    need[s2] = max(need.get(s2, 0), v)
            for s2 in sorted(need):
                self._wait(eng, s2, need[s2])
            if i == nall - 1:
                self.cnt[sem] += 1
                self.ops[eng].append(("o", fn, sem, 1))
            else:
                self.ops[eng].append(("o", fn, None, 0))
        v = self.cnt[sem]
        for b in writes:
            b.writers[sem] = max(b.writers.get(sem, 0), v)
        for i, (fn, reads) in enumerate(steps):
            for b in reads:
                b.readers[sem] = max(b.readers.get(sem, 0), v)

    def _emit_dma(self, eng, fns, reads, writes, key):
        sem = self.sem(("DG_" if eng == "gp" else "DS_") + key)
        need = {}
        for b in reads:
            for s2, v in b.writers.items():
                need[s2] = max(need.get(s2, 0), v)
        for b in writes:
            for s2, v in b.writers.items():
                need[s2] = max(need.get(s2, 0), v)
            for s2, v in b.readers.items():
                need[s2] = max(need.get(s2, 0), v)
        for s2 in sorted(need):
            self._wait(eng, s2, need[s2])
        for fn in fns:
            self.cnt[sem] += 16
            self.ops[eng].append(("o", fn, sem, 16))
        v = self.cnt[sem]
        ws = set(id(b) for b in writes)
        for b in writes:
            b.writers[sem] = max(b.writers.get(sem, 0), v)
        for b in reads:
            if id(b) not in ws:
                b.readers[sem] = max(b.readers.get(sem, 0), v)

    def barrier(self):
        snap = dict(self.cnt)
        for e in ENGINES:
            if e in ("gp", "te"):
                continue
            for s in sorted(snap):
                if s.startswith("DG_"):
                    continue
                self._wait(e, s, snap[s])

    def finalize(self):
        semh = self.semh
        nc = self.nc
        blk = nc.Block()
        block = blk.__enter__()
        deco = {"te": block.tensor, "act": block.scalar, "dve": block.vector,
                "sync": block.sync, "gp": block.gpsimd}

        def run(eng):
            def body(h):
                for item in self.ops[eng]:
                    if item[0] == "w":
                        h.wait_ge(semh[item[1]], item[2])
                    else:
                        inst = item[1](h)
                        if item[2] is not None:
                            inst.then_inc(semh[item[2]], item[3])
            return body
        for e in ENGINES:
            deco[e](run(e))
        blk.__exit__(None, None, None)


class Scope:

    def __init__(self, nc):
        self.nc = nc
        self.stack = []
        self.n = 0

    def sbuf(self, name, shape, dt):
        self.n += 1
        cm = self.nc.sbuf_tensor(f"{name}_{self.n}", shape, dt)
        t = cm.__enter__()
        self.stack.append(cm)
        return t

    def psum(self, name, shape, dt):
        self.n += 1
        cm = self.nc.psum_tensor(f"{name}_{self.n}", shape, dt)
        t = cm.__enter__()
        self.stack.append(cm)
        return t

    def mark(self):
        return len(self.stack)

    def release(self, mark):
        while len(self.stack) > mark:
            cm = self.stack.pop()
            cm.__exit__(None, None, None)


def _host_layouts(inputs, b):
    g = {}
    x = np.asarray(inputs["L0_attn_a_x"])[b]
    g["x_in"] = np.ascontiguousarray(x.T.reshape(NDC, 128, S))
    return g


def _host_shared(inputs):
    g = {}

    def qkv_t(w):
        a = np.asarray(w).transpose(0, 2, 1).reshape(NH, NDC, 128, DH)
        return np.ascontiguousarray(a.transpose(0, 2, 1, 3)).astype(BF)

    def oproj_t(wt):
        a = np.asarray(wt).reshape(NH, DH, NDC, 128)
        return np.ascontiguousarray(a.transpose(2, 0, 1, 3)).astype(BF)

    g["wqa"] = qkv_t(inputs["L0_attn_a_W_Q"])
    g["wka"] = qkv_t(inputs["L0_attn_a_W_K"])
    g["wva"] = qkv_t(inputs["L0_attn_a_W_V"])
    g["woa"] = oproj_t(np.asarray(inputs["L0_attn_a_W_O"]).transpose(0, 2, 1))

    g["wqb"] = qkv_t(np.asarray(inputs["L0_attn_b_Wq"]).reshape(NH, 128, D))
    g["wkb"] = qkv_t(np.asarray(inputs["L0_attn_b_Wk"]).reshape(NH, 128, D))
    g["wvb"] = qkv_t(np.asarray(inputs["L0_attn_b_Wv"]).reshape(NH, 128, D))
    g["wob"] = oproj_t(np.asarray(inputs["L0_attn_b_Wo"]).T.reshape(NH, DH, D))

    bq = np.asarray(inputs["L0_attn_b_bq"]).reshape(NH, 128)
    bk = np.asarray(inputs["L0_attn_b_bk"]).reshape(NH, 128)
    bv = np.asarray(inputs["L0_attn_b_bv"]).reshape(NH, 128)
    g["bqkv"] = np.ascontiguousarray(
        np.concatenate([bq, bk, bv], axis=0).T)
    g["bo"] = np.ascontiguousarray(
        np.asarray(inputs["L0_attn_b_bo"]).reshape(NDC, 128).T)

    def gate_t(w):
        a = np.asarray(w).T.reshape(NDC, 128, NIC, 128)
        return np.ascontiguousarray(a.transpose(2, 1, 0, 3))

    def down_t(w):
        a = np.asarray(w).T.reshape(NIC, 128, NDC, 128)
        return np.ascontiguousarray(a.transpose(2, 1, 0, 3))

    g["sg_t"] = gate_t(inputs["L0_ffn_shared_gate"]).astype(BF)
    g["su_t"] = gate_t(inputs["L0_ffn_shared_up"]).astype(BF)
    g["sd_t"] = down_t(inputs["L0_ffn_shared_down"]).astype(BF)

    wg = np.asarray(inputs["L0_ffn_w_gate"]).astype(BF)
    wu = np.asarray(inputs["L0_ffn_w_up"]).astype(BF)
    wd = np.asarray(inputs["L0_ffn_w_down"]).astype(BF)
    g["wg_t"] = np.stack([gate_t(wg[e]) for e in range(NE)])
    g["wu_t"] = np.stack([gate_t(wu[e]) for e in range(NE)])
    g["wd_t"] = np.stack([down_t(wd[e]) for e in range(NE)])

    rw = np.asarray(inputs["L0_ffn_router_w"]).T.reshape(NDC, 128, NE)
    g["rw"] = np.ascontiguousarray(rw).astype(BF)

    g["c_ident_bf"] = np.eye(128, dtype=BF)
    g["c_ident_f32"] = np.eye(8, 128, dtype=np.float32)
    g["c_ones_rep"] = np.ones((128, 128), dtype=BF)
    g["c_mask_t"] = np.where(np.arange(128)[:, None] > np.arange(128)[None, :],
                             np.float32(-1e9), np.float32(0.0))
    sel = np.zeros((8, NE * 128), dtype=BF)
    for e in range(NE):
        sel[e, e * 128:(e + 1) * 128] = 1.0
    g["c_sel"] = sel
    inv = 1.0 / (10000.0 ** (np.arange(0, DH, 2, dtype=np.float64) / DH))
    t = np.arange(S, dtype=np.float64)
    fr = t[None, :] * np.concatenate([inv, inv])[:, None]
    g["c_cos"] = np.cos(fr).astype(np.float32)
    g["c_eps"] = np.full((128, 1), EPS, dtype=np.float32)
    sin = np.sin(fr)
    sin[64:128] *= -1.0
    g["c_sin"] = sin.astype(np.float32)
    return g


def build_program(phases="all"):
    nc = bass.Bass("TRN2", target_bir_lowering=False)
    sc = Scope(nc)
    kb = KB(nc)

    dt_in = {}

    def din(name, shape, dt):
        dt_in[name] = nc.dram_tensor(name, shape, dt, kind="ExternalInput").ap()
        return dt_in[name]

    x_in = din("x_in", [NDC, 128, S], F32)
    wqa = din("wqa", [NH, 128, NDC, DH], BF16)
    wka = din("wka", [NH, 128, NDC, DH], BF16)
    wva = din("wva", [NH, 128, NDC, DH], BF16)
    woa = din("woa", [NDC, NH, DH, 128], BF16)
    wqb = din("wqb", [NH, 128, NDC, DH], BF16)
    wkb = din("wkb", [NH, 128, NDC, DH], BF16)
    wvb = din("wvb", [NH, 128, NDC, DH], BF16)
    wob = din("wob", [NDC, NH, DH, 128], BF16)
    bqkv = din("bqkv", [128, 48], F32)
    bo_in = din("bo", [128, 16], F32)
    sg_t = din("sg_t", [NIC, 128, NDC, 128], BF16)
    su_t = din("su_t", [NIC, 128, NDC, 128], BF16)
    sd_t = din("sd_t", [NDC, 128, NIC, 128], BF16)
    wg_t = din("wg_t", [NE, NIC, 128, NDC, 128], BF16)
    wu_t = din("wu_t", [NE, NIC, 128, NDC, 128], BF16)
    wd_t = din("wd_t", [NE, NDC, 128, NIC, 128], BF16)
    rw_in = din("rw", [NDC, 128, NE], BF16)
    c_ident_bf = din("c_ident_bf", [128, 128], BF16)
    c_ident_f32 = din("c_ident_f32", [8, 128], F32)
    c_ones_rep = din("c_ones_rep", [128, 128], BF16)
    c_mask_t = din("c_mask_t", [128, 128], F32)
    c_sel = din("c_sel", [8, NE * 128], BF16)
    c_cos = din("c_cos", [128, S], F32)
    c_sin = din("c_sin", [128, S], F32)
    c_eps = din("c_eps", [128, 1], F32)

    y_out = nc.dram_tensor("y", [NDC, 128, S], F32, kind="ExternalOutput").ap()

    QS = nc.dram_tensor("QS", [NH, 128, S], BF16, kind="Internal").ap()
    KS = nc.dram_tensor("KS", [NH, 128, S], BF16, kind="Internal").ap()
    VS = nc.dram_tensor("VS", [NH, 16, 128, DH], BF16, kind="Internal").ap()
    OS = nc.dram_tensor("OS", [NH, 128, S], BF16, kind="Internal").ap()

    x_fm = sc.sbuf("x_fm", [128, NDC * S], F32)
    ident_bf = sc.sbuf("ident_bf", [128, 128], BF16)
    ones_rep = sc.sbuf("ones_rep", [128, 128], BF16)
    mask_t = sc.sbuf("mask_t", [128, 128], F32)
    rw_sb = sc.sbuf("rw_sb", [128, NDC * NE], BF16)
    bqkv_sb = sc.sbuf("bqkv_sb", [128, 48], F32)
    bo_sb = sc.sbuf("bo_sb", [128, 16], F32)
    eps_sb = sc.sbuf("eps_sb", [128, 1], F32)
    wpool = [sc.sbuf(f"wp{i}", [128, 2048], BF16) for i in range(4)]
    wdpool = [sc.sbuf(f"wdp{i}", [128, 6 * 128], BF16) for i in range(2)]
    ostp = [sc.sbuf(f"ostp{i}", [128, 512], BF16) for i in range(2)]
    w8 = sc.sbuf("r_w8", [128, 8], F32)
    t8a = sc.sbuf("r_t8a", [128, 8], F32)
    t8b = sc.sbuf("r_t8b", [128, 8], F32)
    c8 = sc.sbuf("r_c8", [128, 8], BF16)
    nm8 = sc.sbuf("r_nm8", [128, 1], F32)
    sm8 = sc.sbuf("r_sm8", [128, 1], F32)
    rc8 = sc.sbuf("r_rc8", [128, 1], F32)

    pbw = [sc.psum(f"pbw{i}", [128, 1024], F32) for i in range(3)]
    pb7 = sc.psum("pb7", [128, 512], F32)
    pbt = sc.psum("pbt", [128, 512], BF16)
    pb = [pbw[0][:, 0:512], pbw[0][:, 512:1024],
          pbw[1][:, 0:512], pbw[1][:, 512:1024],
          pbw[2][:, 0:512], pbw[2][:, 512:1024], pb7[:]]

    bx = [Buf(f"x{dc}") for dc in range(NDC)]
    bpbw = [Buf(f"pbw{i}") for i in range(3)]
    bpb = [Buf(f"pb{i}") for i in range(7)]
    bpbt = Buf("pbt")
    bconst = Buf("const")
    bwpool = [Buf(f"wp{i}") for i in range(4)]
    bwdpool = [Buf(f"wdp{i}") for i in range(2)]
    bostp = [Buf(f"ostp{i}") for i in range(2)]
    bQS = [Buf(f"QS{h}") for h in range(NH)]
    bKS = [Buf(f"KS{h}") for h in range(NH)]
    bVS = [Buf(f"VS{h}") for h in range(NH)]
    bOS = [Buf(f"OS{h}") for h in range(NH)]

    def dma(eng, out_ap, in_ap, reads=(), writes=(), key="misc"):
        kb.emit(eng, (lambda h, o=out_ap, i=in_ap: h.dma_start(out=o, in_=i)),
                reads=reads, writes=writes, dma_key=key)

    dma("sync", ident_bf[:], c_ident_bf, writes=[bconst], key="const")
    dma("sync", ones_rep[:], c_ones_rep, writes=[bconst], key="const")
    dma("sync", mask_t[:], c_mask_t, writes=[bconst], key="const")
    dma("gp", rw_sb[:].rearrange("p (dc e) -> p dc e", dc=NDC),
        rw_in.rearrange("dc p e -> p dc e"), writes=[bconst], key="const")
    dma("sync", bqkv_sb[:], bqkv, writes=[bconst], key="const")
    dma("sync", bo_sb[:], bo_in, writes=[bconst], key="const")
    dma("sync", eps_sb[:], c_eps, writes=[bconst], key="const")
    for i in range(8):
        dma("sync" if i % 2 == 0 else "gp",
            x_fm[:, i * 2 * S:(i + 1) * 2 * S].rearrange("p (dc t) -> p dc t", dc=2),
            x_in[i * 2:(i + 1) * 2].rearrange("dc p t -> p dc t"),
            writes=[bx[dc] for dc in range(2 * i, 2 * i + 2)],
            key=f"xin{i // 2 % 2}")

    def xs(dc, t0, n=TT):
        return x_fm[:, dc * S + t0: dc * S + t0 + n]

    def emit_ln(H, m_b, rstd_b, bm, brs, h_half, bh, lnsc, blnsc, xbq, bxbq,
               add_skip):
        T0 = H * HALF
        for tt in range(2):
            t0 = T0 + tt * TT
            psx, bsx = pb[0], bpb[0]
            psq, bsq = pb[1], bpb[1]
            for dc in range(NDC):
                xb_t, bxb_t = xbq[dc % 2], bxbq[dc % 2]
                sq_t, bsq_t = xbq[2 + dc % 2], bxbq[2 + dc % 2]
                src = xs(dc, t0)
                flip = dc % 3 == 1
                kb.emit("act" if flip else "dve",
                        (lambda h, o=xb_t, i=src:
                         h.activation(o[:], i, AF.Copy)) if flip else
                        (lambda h, o=xb_t, i=src: h.tensor_copy(o[:], i)),
                        reads=[bx[dc]], writes=[bxb_t])
                kb.emit("dve" if flip else "act",
                        (lambda h, o=sq_t, i=src:
                         h.tensor_tensor(o[:], i, i, ALU.mult)) if flip else
                        (lambda h, o=sq_t, i=src:
                         h.activation(o[:], i, AF.Square)),
                        reads=[bx[dc]], writes=[bsq_t])
                last = dc == NDC - 1
                kb.emit("te", (lambda h, o=psx, i=xb_t, st=(dc == 0), sp=last:
                               h.matmul(o[:], ones_rep[:], i[:], start=st, stop=sp)),
                        reads=[bxb_t, bconst], writes=[bsx])
                kb.emit("te", (lambda h, o=psq, i=sq_t, st=(dc == 0), sp=last:
                               h.matmul(o[:], ones_rep[:], i[:], start=st, stop=sp)),
                        reads=[bsq_t, bconst], writes=[bsq])
            mslice = m_b[:, 0:TT]
            rslice = rstd_b[:, 0:TT]
            kb.emit("dve", (lambda h, o=mslice, i=psx:
                            h.tensor_scalar(o, i[:], 1.0 / D, None, ALU.mult)),
                    reads=[bsx], writes=[bm])
            kb.emit("dve", (lambda h, o=lnsc[0], i=psq:
                            h.tensor_scalar(o[:], i[:], 1.0 / D, None, ALU.mult)),
                    reads=[bsq], writes=[blnsc[0]])
            kb.emit("dve", (lambda h, o=lnsc[1], a=mslice:
                            h.tensor_tensor(o[:], a, a, ALU.mult)),
                    reads=[bm], writes=[blnsc[1]])
            kb.emit("dve", (lambda h, o=lnsc[0], a=lnsc[0], b2=lnsc[1]:
                            h.tensor_tensor(o[:], a[:], b2[:], ALU.subtract)),
                    reads=[blnsc[0], blnsc[1]], writes=[blnsc[0]])
            kb.emit("act", (lambda h, o=lnsc[1], i=lnsc[0]:
                            h.activation(o[:], i[:], AF.Ln,
                                         bias=eps_sb[:, 0:1])),
                    reads=[blnsc[0], bconst], writes=[blnsc[1]])
            kb.emit("act", (lambda h, o=rslice, i=lnsc[1]:
                            h.activation(o, i[:], AF.Exp, scale=-0.5)),
                    reads=[blnsc[1]], writes=[brs])
            for dc in range(NDC):
                src = xs(dc, t0)
                hsl = h_half[:, dc * HALF + tt * TT: dc * HALF + (tt + 1) * TT]
                kb.emit("dve", (lambda h, o=lnsc[0], a=src, b2=mslice:
                                h.tensor_tensor(o[:], a, b2, ALU.subtract)),
                        reads=[bx[dc], bm], writes=[blnsc[0]])
                if not add_skip:
                    kb.emit("dve", (lambda h, o=hsl, a=lnsc[0], b2=rslice:
                                    h.tensor_tensor(o, a[:], b2, ALU.mult)),
                            reads=[blnsc[0], brs], writes=[bh[(dc, tt)]])
                else:
                    kb.emit("dve", (lambda h, o=lnsc[1], a=lnsc[0], b2=rslice:
                                    h.tensor_tensor(o[:], a[:], b2, ALU.mult)),
                            reads=[blnsc[0], brs], writes=[blnsc[1]])
                    kb.emit("act", (lambda h, o=hsl, i=lnsc[1]:
                                    h.activation(o, i[:], AF.Copy)),
                            reads=[blnsc[1]], writes=[bh[(dc, tt)]])
                    kb.emit("dve", (lambda h, o=src, a=src, b2=lnsc[1]:
                                    h.tensor_tensor(o, a, b2[:], ALU.add)),
                            reads=[bx[dc], blnsc[1]], writes=[bx[dc]])

    def emit_attn(lay):
        m = sc.mark()
        h_half = sc.sbuf("a_h", [128, NDC * HALF], BF16)
        m_b = sc.sbuf("a_m", [128, TT], F32)
        rstd_b = sc.sbuf("a_r", [128, TT], F32)
        lnsc = [sc.sbuf(f"a_sc{i}", [128, 512], F32) for i in range(2)]
        xbq = [sc.sbuf(f"a_xb{i}", [128, 512], BF16) for i in range(4)]
        stg = [sc.sbuf(f"a_st{i}", [128, 512], BF16) for i in range(2)]
        vstg = [sc.sbuf(f"a_vs{i}", [128, 512], BF16) for i in range(2)]
        if lay["rope"]:
            cosb = [sc.sbuf(f"a_cos{i}", [128, TT], F32) for i in range(2)]
            sinb = [sc.sbuf(f"a_sin{i}", [128, TT], F32) for i in range(2)]
        bm, brs = Buf("m"), Buf("r")
        bh = {(dc, tt): Buf(f"h{dc}_{tt}") for dc in range(NDC) for tt in range(2)}
        blnsc = [Buf(f"sc{i}") for i in range(2)]
        bxbq = [Buf(f"xb{i}") for i in range(4)]
        bstg = [Buf(f"st{i}") for i in range(2)]
        bvstg = [Buf(f"vs{i}") for i in range(2)]
        bcos = [Buf(f"cos{i}") for i in range(2)]
        bsin = [Buf(f"sin{i}") for i in range(2)]

        wi = 0
        si = 0
        vi = 0
        for H in range(2):
            T0 = H * HALF
            if lay["rope"]:
                for tt in range(2):
                    tx = T0 + tt * TT
                    dma("sync", cosb[tt][:], c_cos[:, tx:tx + TT],
                        writes=[bcos[tt]], key=f"cos{tt}")
                    dma("sync", sinb[tt][:], c_sin[:, tx:tx + TT],
                        writes=[bsin[tt]], key=f"sin{tt}")
            emit_ln(H, m_b, rstd_b, bm, brs, h_half, bh, lnsc, blnsc,
                    xbq, bxbq, add_skip=False)
            for hd in range(NH):
                for mi, (wdr, outdr, boutb) in enumerate(
                        [(lay["wq"], QS, bQS), (lay["wk"], KS, bKS),
                         (lay["wv"], VS, bVS)]):
                    w_t, bw = wpool[wi % 4], bwpool[wi % 4]
                    wkey = f"wp{wi % 4}"
                    wi += 1
                    dma("gp", w_t[:].rearrange("p (dc e) -> p dc e", dc=NDC),
                        wdr[hd], writes=[bw], key=wkey)
                    for tt in range(2):
                        t0 = T0 + tt * TT
                        ps, bps = pb[2 + (tt + 2 * mi) % 2], bpb[2 + (tt + 2 * mi) % 2]
                        steps = []
                        for dc in range(NDC):
                            steps.append((lambda h, o=ps,
                                          w=w_t[:, dc * DH:(dc + 1) * DH],
                                          r=h_half[:, dc * HALF + tt * TT:
                                                   dc * HALF + (tt + 1) * TT],
                                          st=(dc == 0), sp=(dc == NDC - 1):
                                          h.matmul(o[:], w, r, start=st, stop=sp),
                                          [bw, bh[(dc, tt)]]))
                        kb.emit_seq("te", steps, writes=[bps])
                        is_v = mi == 2
                        if lay["bias"]:
                            bias_ap = bqkv_sb[:, mi * 16 + hd: mi * 16 + hd + 1]
                        if not is_v:
                            if lay["rope"]:
                                qf, bqf = lnsc[0], blnsc[0]
                                kb.emit("act", (lambda h, o=qf, i=ps, ba=bias_ap:
                                                h.activation(o[:], i[:], AF.Identity,
                                                             bias=ba))
                                        if lay["bias"] else
                                        (lambda h, o=qf, i=ps:
                                         h.activation(o[:], i[:], AF.Copy)),
                                        reads=[bps, bconst], writes=[bqf])
                                t1, bt1 = lnsc[1], blnsc[1]
                                qc, bqc = qf, bqf
                                cs = cosb[tt][:, :]
                                sn = sinb[tt][:, :]
                                kb.emit("dve",
                                        [(lambda h, o=t1, a=qf, s2=sn:
                                          h.tensor_tensor(o[0:64, :], a[64:128, :],
                                                          s2[64:128, :], ALU.mult)),
                                         (lambda h, o=t1, a=qf, s2=sn:
                                          h.tensor_tensor(o[64:128, :], a[0:64, :],
                                                          s2[0:64, :], ALU.mult))],
                                        reads=[bqf, bsin[tt]], writes=[bt1])
                                kb.emit("dve", (lambda h, o=qc, a=qf, c=cs:
                                                h.tensor_tensor(o[:], a[:], c,
                                                                ALU.mult)),
                                        reads=[bqf, bcos[tt]], writes=[bqc])
                                st_t, bst = stg[si % 2], bstg[si % 2]
                                si += 1
                                kb.emit("dve", (lambda h, o=st_t, a=qc, b2=t1:
                                                h.tensor_tensor(o[:], a[:], b2[:],
                                                                ALU.add)),
                                        reads=[bqc, bt1], writes=[bst])
                            else:
                                st_t, bst = stg[si % 2], bstg[si % 2]
                                si += 1
                                kb.emit("act", (lambda h, o=st_t, i=ps:
                                                h.activation(o[:], i[:], AF.Copy)),
                                        reads=[bps], writes=[bst])
                            dma("sync", outdr[hd][:, t0:t0 + TT], st_t[:],
                                reads=[bst], writes=[boutb[hd]],
                                key=f"stg{(si - 1) % 2}")
                        else:
                            vf, bvf = vstg[vi % 2], bvstg[vi % 2]
                            kb.emit("act", (lambda h, o=vf, i=ps, ba=bias_ap:
                                            h.activation(o[:], i[:], AF.Identity,
                                                         bias=ba))
                                    if lay["bias"] else
                                    (lambda h, o=vf, i=ps:
                                     h.activation(o[:], i[:], AF.Copy)),
                                    reads=[bps, bconst], writes=[bvf])
                            fns = [(lambda h, o=pbt[:, s * 128:(s + 1) * 128],
                                    i=vf[:, s * 128:(s + 1) * 128]:
                                    h.transpose(o, i, ident_bf[:]))
                                   for s in range(4)]
                            kb.emit("te", fns, reads=[bvf, bconst], writes=[bpbt])
                            vt, bvt = vstg[(vi + 1) % 2], bvstg[(vi + 1) % 2]
                            vi += 2
                            kb.emit("act", (lambda h, o=vt, i=pbt:
                                            h.activation(o[:], i[:], AF.Copy)),
                                    reads=[bpbt], writes=[bvt])
                            kc0 = t0 // 128
                            dma("sync",
                                VS[hd, kc0:kc0 + 4].rearrange("s p e -> p s e"),
                                vt[:].rearrange("p (s e) -> p s e", s=4),
                                reads=[bvt], writes=[bVS[hd]],
                                key=f"vstg{(vi - 1) % 2}")
        kb.barrier()
        sc.release(m)

        m = sc.mark()
        qh = [sc.sbuf(f"b_q{i}", [128, S], BF16) for i in range(2)]
        kh = [sc.sbuf(f"b_k{i}", [128, S], BF16) for i in range(2)]
        vh = [sc.sbuf(f"b_v{i}", [128, 16 * DH], BF16) for i in range(2)]
        pbuf = sc.sbuf("b_p", [128, 16 * 512], BF16)
        rsc = [sc.sbuf(f"b_r{i}", [128, 512], F32) for i in range(2)]
        bqh = [Buf(f"q{i}") for i in range(2)]
        bkh = [Buf(f"k{i}") for i in range(2)]
        bvh = [Buf(f"v{i}") for i in range(2)]
        bpk = [Buf(f"p{kc}") for kc in range(16)]
        brsc = [Buf(f"r{i}") for i in range(2)]
        oi = 0
        for hd in range(NH):
            sl = hd % 2
            dma("sync", qh[sl][:], QS[hd], reads=[bQS[hd]], writes=[bqh[sl]],
                key=f"qh{sl}")
            dma("sync", kh[sl][:], KS[hd], reads=[bKS[hd]], writes=[bkh[sl]],
                key=f"kh{sl}")
            for half8 in range(2):
                dma("sync",
                    vh[sl][:, half8 * 8 * DH:(half8 + 1) * 8 * DH]
                    .rearrange("p (s e) -> p s e", s=8),
                    VS[hd, half8 * 8:(half8 + 1) * 8].rearrange("s p e -> p s e"),
                    reads=[bVS[hd]], writes=[bvh[sl]], key=f"vh{sl}")
            for qb in range(4):
                kcm = (qb + 1) * 4 if lay["causal"] else 16
                q_ap = qh[sl][:, qb * 512:(qb + 1) * 512]
                for kc2 in range(0, kcm, 2):
                    wide, bwide = pbw[kc2 % 4 // 2], bpbw[kc2 % 4 // 2]
                    for ki in range(2):
                        kc = kc2 + ki
                        ps = wide[:, ki * 512:(ki + 1) * 512]
                        kb.emit("te", (lambda h, o=ps,
                                       w=kh[sl][:, kc * 128:(kc + 1) * 128],
                                       r=q_ap:
                                       h.matmul(o, w, r, start=True, stop=True)),
                                reads=[bkh[sl], bqh[sl]], writes=[bwide])
                        if lay["causal"] and kc >= qb * 4:
                            jj = kc - qb * 4
                            kb.emit("dve", (lambda h, o=ps, jj=jj:
                                            h.tensor_tensor(
                                                o[:, jj * 128:(jj + 1) * 128],
                                                o[:, jj * 128:(jj + 1) * 128],
                                                mask_t[:], ALU.add)),
                                    reads=[bwide, bconst], writes=[bwide])
                    psl = pbuf[:, kc2 * 512:(kc2 + 2) * 512]
                    kb.emit("act", (lambda h, o=psl, i=wide:
                                    h.activation(o, i[:], AF.Exp, scale=SCALE)),
                            reads=[bwide], writes=[bpk[kc2], bpk[kc2 + 1]])
                    for ki in range(2):
                        kc = kc2 + ki
                        if lay["causal"] and kc >= qb * 4:
                            jj = kc - qb * 4
                            if jj > 0:
                                kb.emit("dve", (lambda h,
                                                o=pbuf[:, kc * 512:(kc + 1) * 512],
                                                jj=jj:
                                                h.memset(o[:, 0:jj * 128], 0.0)),
                                        reads=[], writes=[bpk[kc]])
                pssum, bsum = pb[4], bpb[4]
                psout, bout = pb[5], bpb[5]
                fns_s, fns_v = [], []
                for kc in range(kcm):
                    psl = pbuf[:, kc * 512:(kc + 1) * 512]
                    fns_s.append(lambda h, o=pssum, r=psl, st=(kc == 0),
                                 sp=(kc == kcm - 1):
                                 h.matmul(o[:], ones_rep[:], r, start=st, stop=sp))
                    fns_v.append(lambda h, o=psout,
                                 w=vh[sl][:, kc * 128:(kc + 1) * 128], r=psl,
                                 st=(kc == 0), sp=(kc == kcm - 1):
                                 h.matmul(o[:], w, r, start=st, stop=sp))
                kb.emit("te", fns_s, reads=[bconst] + [bpk[k] for k in range(kcm)],
                        writes=[bsum])
                kb.emit("te", fns_v, reads=[bvh[sl]] + [bpk[k] for k in range(kcm)],
                        writes=[bout])
                r1, br1 = rsc[1], brsc[1]
                kb.emit("dve", (lambda h, o=r1, i=pssum:
                                h.reciprocal(o[:], i[:])),
                        reads=[bsum], writes=[br1])
                o_t, bo_t = ostp[oi % 2], bostp[oi % 2]
                oi += 1
                kb.emit("dve", (lambda h, o=o_t, a=psout, b2=r1:
                                h.tensor_tensor(o[:], a[:], b2[:], ALU.mult)),
                        reads=[bout, br1], writes=[bo_t])
                dma("gp", OS[hd][:, qb * 512:(qb + 1) * 512], o_t[:],
                    reads=[bo_t], writes=[bOS[hd]], key=f"ost{(oi - 1) % 2}")
        kb.barrier()
        sc.release(m)

        m = sc.mark()
        og = sc.sbuf("c_og", [128, 8 * S], BF16)
        bog8 = [Buf(f"og{i}") for i in range(8)]
        for g in range(2):
            for i8 in range(8):
                hd = g * 8 + i8
                dma("sync", og[:, i8 * S:(i8 + 1) * S], OS[hd],
                    reads=[bOS[hd]], writes=[bog8[i8]], key=f"og{i8 % 2}")
            for dc in range(NDC):
                w_t, bw = wpool[dc % 4], bwpool[dc % 4]
                dma("gp", w_t[:, 0:1024].rearrange("p (hh d) -> p hh d", hh=8),
                    lay["wo"][dc, g * 8:(g + 1) * 8].rearrange("hh p d -> p hh d"),
                    writes=[bw], key=f"wp{dc % 4}")
                for tt in range(4):
                    ps, bps = pb[2 + tt % 2], bpb[2 + tt % 2]
                    steps = []
                    for i8 in range(8):
                        steps.append((lambda h, o=ps,
                                      w=w_t[:, i8 * 128:(i8 + 1) * 128],
                                      r=og[:, i8 * S + tt * TT:
                                           i8 * S + (tt + 1) * TT],
                                      st=(i8 == 0), sp=(i8 == 7):
                                      h.matmul(o[:], w, r, start=st, stop=sp),
                                      [bw, bog8[i8]]))
                    kb.emit_seq("te", steps, writes=[bps])
                    dst = xs(dc, tt * TT)
                    kb.emit("dve", (lambda h, o=dst, i=ps:
                                    h.tensor_tensor(o, o, i[:], ALU.add)),
                            reads=[bx[dc], bps], writes=[bx[dc]])
        if lay["bias"]:
            for dc in range(NDC):
                kb.emit("dve", (lambda h, o=x_fm[:, dc * S:(dc + 1) * S], dc=dc:
                                h.tensor_scalar(o, o, bo_sb[:, dc:dc + 1], None,
                                                ALU.add)),
                        reads=[bx[dc], bconst], writes=[bx[dc]])
        kb.barrier()
        sc.release(m)

    def emit_ffn():
        for H in range(2):
            m = sc.mark()
            h_half = sc.sbuf("f_h", [128, NDC * HALF], BF16)
            bh = {(dc, tt): Buf(f"fh{dc}_{tt}")
                  for dc in range(NDC) for tt in range(2)}
            comb_fm = sc.sbuf("f_cf", [8, HALF], BF16)
            sel_sb = sc.sbuf("f_sel", [8, NE * 128], BF16)
            ident_f32 = sc.sbuf("f_idf", [8, 128], F32)
            bsel = Buf("sel")
            dma("sync", sel_sb[:], c_sel, writes=[bsel], key="sel")
            dma("sync", ident_f32[:], c_ident_f32, writes=[bsel], key="sel")
            lg_s = sc.sbuf("f_lg", [8, 1024], F32)
            m0 = sc.mark()
            m_b = sc.sbuf("f_m", [128, TT], F32)
            rstd_b = sc.sbuf("f_r", [128, TT], F32)
            lnsc = [sc.sbuf(f"f_sc{i}", [128, 512], F32) for i in range(3)]
            xbq = [sc.sbuf(f"f_xb{i}", [128, 512], BF16) for i in range(4)]
            bm, brs = Buf("fm"), Buf("fr")
            blnsc = [Buf(f"fsc{i}") for i in range(3)]
            bxbq = [Buf(f"fxb{i}") for i in range(4)]
            emit_ln(H, m_b, rstd_b, bm, brs, h_half, bh, lnsc, blnsc, xbq, bxbq,
                    add_skip=("skip" in _FFN_PARTS))
            bcf, blg = Buf("cf"), Buf("lg")
            bw8, bt8a, bt8b, bc8, bnm8, bsm8, brc8 = (
                Buf("w8"), Buf("t8a"), Buf("t8b"), Buf("c8"), Buf("nm8"),
                Buf("sm8"), Buf("rc8"))
            for tt in (range(2) if "router" in _FFN_PARTS else ()):
                t0 = H * HALF + tt * TT
                ps, bps = pb[6], bpb[6]
                steps = []
                for dc in range(NDC):
                    steps.append((lambda h, o=ps, w=rw_sb[:, dc * NE:(dc + 1) * NE],
                                  r=h_half[:, dc * HALF + tt * TT:
                                           dc * HALF + (tt + 1) * TT],
                                  st=(dc == 0), sp=(dc == NDC - 1):
                                  h.matmul(o[0:8, :], w, r, start=st, stop=sp),
                                  [bconst, bh[(dc, tt)]]))
                kb.emit_seq("te", steps, writes=[bps])
                lsl = lg_s[0:8, tt * TT:(tt + 1) * TT]
                kb.emit("act", (lambda h, o=lsl, i=ps:
                                h.activation(o, i[0:8, :], AF.Copy)),
                        reads=[bps], writes=[blg])

            def emit_router_post(tt):
                for st4 in range(4):
                    ps2, bps2 = pb[5], bpb[5]
                    kb.emit("te", (lambda h, o=ps2,
                                   w=lg_s[0:8, tt * TT + st4 * 128:
                                          tt * TT + (st4 + 1) * 128]:
                                   h.matmul(o[0:128, 0:8], w, ident_f32[0:8, 0:8],
                                            start=True, stop=True)),
                            reads=[blg, bsel], writes=[bps2])
                    kb.emit("dve", (lambda h, o=nm8, i=ps2:
                                    h.reduce_max(o[:], i[0:128, 0:8], axis=AX.X)),
                            reads=[bps2], writes=[bnm8])
                    kb.emit("dve", (lambda h, o=nm8, i=nm8:
                                    h.tensor_scalar(o[:], i[:], -1.0, None,
                                                    ALU.mult)),
                            reads=[bnm8], writes=[bnm8])
                    kb.emit("act", (lambda h, o=w8, i=ps2:
                                    h.activation(o[:], i[0:128, 0:8], AF.Exp,
                                                 bias=nm8[:, 0:1],
                                                 accum_out=sm8[:])),
                            reads=[bps2, bnm8], writes=[bw8, bsm8])
                    kb.emit("dve", (lambda h, o=rc8, i=sm8:
                                    h.reciprocal(o[:], i[:])),
                            reads=[bsm8], writes=[brc8])
                    kb.emit("dve", (lambda h, o=w8, i=w8:
                                    h.tensor_scalar(o[:], i[:], rc8[:, 0:1], None,
                                                    ALU.mult)),
                            reads=[bw8, brc8], writes=[bw8])
                    kb.emit("dve", (lambda h, o=nm8, i=w8:
                                    h.reduce_max(o[:], i[:], axis=AX.X)),
                            reads=[bw8], writes=[bnm8])
                    kb.emit("dve", (lambda h, o=t8a, i=w8:
                                    h.tensor_scalar(o[:], i[:], nm8[:, 0:1], None,
                                                    ALU.is_ge)),
                            reads=[bw8, bnm8], writes=[bt8a])
                    kb.emit("dve", (lambda h, o=t8a, i=t8a:
                                    h.tensor_scalar(o[:], i[:], 1e9, None,
                                                    ALU.mult)),
                            reads=[bt8a], writes=[bt8a])
                    kb.emit("dve", (lambda h, o=t8b, a=w8, b2=t8a:
                                    h.tensor_tensor(o[:], a[:], b2[:],
                                                    ALU.subtract)),
                            reads=[bw8, bt8a], writes=[bt8b])
                    kb.emit("dve", (lambda h, o=nm8, i=t8b:
                                    h.reduce_max(o[:], i[:], axis=AX.X)),
                            reads=[bt8b], writes=[bnm8])
                    kb.emit("dve", (lambda h, o=t8a, i=w8:
                                    h.tensor_scalar(o[:], i[:], nm8[:, 0:1], None,
                                                    ALU.is_ge)),
                            reads=[bw8, bnm8], writes=[bt8a])
                    kb.emit("dve", (lambda h, o=c8, a=w8, b2=t8a:
                                    h.tensor_tensor(o[:], a[:], b2[:], ALU.mult)),
                            reads=[bw8, bt8a], writes=[bc8])
                    kb.emit("te", (lambda h, o=pb[4], w=c8:
                                   h.matmul(o[0:8, 0:128], w[:], ident_bf[:],
                                            start=True, stop=True)),
                            reads=[bc8, bconst], writes=[bpb[4]])
                    csl = comb_fm[0:8, tt * TT + st4 * 128: tt * TT + (st4 + 1) * 128]
                    kb.emit("act", (lambda h, o=csl, i=pb[4]:
                                    h.activation(o, i[0:8, 0:128], AF.Copy)),
                            reads=[bpb[4]], writes=[bcf])

            kb.barrier()
            sc.release(m0)
            heb = sc.sbuf("f_heb", [128, 6 * HALF], BF16)
            sgq = [sc.sbuf(f"f_sg{i}", [128, 512], BF16) for i in range(2)]
            cbq = [sc.sbuf(f"f_cb{i}", [128, 512], BF16) for i in range(2)]
            bheb = [Buf(f"hb{i}") for i in range(6)]
            bsgq = [Buf(f"sg{i}") for i in range(2)]
            bcbq = [Buf(f"cb{i}") for i in range(2)]
            gi = 0
            if _ROUTER_EARLY:
                elist = [-1] + ([0] if "shared" in _FFN_PARTS else []) + \
                    (list(range(1, 9)) if "routed" in _FFN_PARTS else [])
            else:
                elist = ([0] if "shared" in _FFN_PARTS else []) + [-1] + \
                    (list(range(1, 9)) if "routed" in _FFN_PARTS else [])
            for e in elist:
                if e == -1:
                    if "router" in _FFN_PARTS:
                        emit_router_post(0)
                        emit_router_post(1)
                    continue
                shared = e == 0
                if not shared:
                    for tt in range(2):
                        cb, bcb = cbq[tt], bcbq[tt]
                        kb.emit("te", (lambda h, o=pb[6],
                                       w=sel_sb[0:8, (e - 1) * 128: e * 128],
                                       r=comb_fm[0:8, tt * TT:(tt + 1) * TT]:
                                       h.matmul(o[:], w, r, start=True, stop=True)),
                                reads=[bsel, bcf], writes=[bpb[6]])
                        kb.emit("act", (lambda h, o=cb, i=pb[6]:
                                        h.activation(o[:], i[:], AF.Copy)),
                                reads=[bpb[6]], writes=[bcb])
                for part, (ic0, ic1) in enumerate([(0, 6), (6, 11)]):
                    for ic in range(ic0, ic1):
                        wg_s, bwg = wpool[gi % 2], bwpool[gi % 2]
                        wu_s, bwu = wpool[2 + gi % 2], bwpool[2 + gi % 2]
                        gi += 1
                        wgkey, wukey = f"wp{(gi - 1) % 2}", f"wp{2 + (gi - 1) % 2}"
                        q = "gp" if shared else "sync"
                        gsrc = (sg_t[ic] if shared else wg_t[e - 1, ic])
                        usrc = (su_t[ic] if shared else wu_t[e - 1, ic])
                        dma(q, wg_s[:].rearrange("p (dc i) -> p dc i", dc=NDC),
                            gsrc, writes=[bwg], key=wgkey)
                        dma(q, wu_s[:].rearrange("p (dc i) -> p dc i", dc=NDC),
                            usrc, writes=[bwu], key=wukey)
                        for tt in range(2):
                            psg, bpsg = pb[2], bpb[2]
                            psu, bpsu = pb[3], bpb[3]
                            for ww, ps, bps, bw in ((wg_s, psg, bpsg, bwg),
                                                    (wu_s, psu, bpsu, bwu)):
                                steps = []
                                for dc in range(NDC):
                                    steps.append((lambda h, o=ps,
                                                  w=ww[:, dc * 128:(dc + 1) * 128],
                                                  r=h_half[:, dc * HALF + tt * TT:
                                                           dc * HALF + (tt + 1) * TT],
                                                  st=(dc == 0), sp=(dc == NDC - 1):
                                                  h.matmul(o[:], w, r,
                                                           start=st, stop=sp),
                                                  [bw, bh[(dc, tt)]]))
                                kb.emit_seq("te", steps, writes=[bps])
                            sg_s, bsg = sgq[tt], bsgq[tt]
                            kb.emit("act", (lambda h, o=sg_s, i=psg:
                                            h.activation(o[:], i[:], AF.Sigmoid)),
                                    reads=[bpsg], writes=[bsg])
                            hslot = ic - ic0
                            hsl = heb[:, hslot * HALF + tt * TT:
                                      hslot * HALF + (tt + 1) * TT]
                            if shared:
                                kb.emit("dve", (lambda h, o=hsl, a=sg_s, b2=psu:
                                                h.tensor_tensor(o, a[:], b2[:],
                                                                ALU.mult)),
                                        reads=[bsg, bpsu], writes=[bheb[hslot]])
                            else:
                                kb.emit("dve", (lambda h, o=hsl, a=sg_s, b2=psu:
                                                h.tensor_tensor(o, a[:], b2[:],
                                                                ALU.mult)),
                                        reads=[bsg, bpsu], writes=[bheb[hslot]])
                                kb.emit("dve", (lambda h, o=hsl, b2=cbq[tt]:
                                                h.tensor_tensor(o, o, b2[:],
                                                                ALU.mult)),
                                        reads=[bheb[hslot], bcbq[tt]],
                                        writes=[bheb[hslot]])
                    nic_p = ic1 - ic0
                    for dc in (range(NDC) if "down" in _FFN_PARTS else ()):
                        wd_s, bwd = wdpool[dc % 2], bwdpool[dc % 2]
                        src = (sd_t if shared else wd_t[e - 1])
                        ap = src[dc][:, ic0:ic1, :]
                        dma("gp" if shared else "sync",
                            wd_s[:, 0:nic_p * 128].rearrange(
                                "p (ic d) -> p ic d", ic=nic_p),
                            ap, writes=[bwd], key=f"wdp{dc % 2}")
                        for tt in range(2):
                            ps, bps = pb[4 + tt % 2], bpb[4 + tt % 2]
                            fns = []
                            for l in range(nic_p):
                                fns.append(lambda h, o=ps,
                                           w=wd_s[:, l * 128:(l + 1) * 128],
                                           r=heb[:, l * HALF + tt * TT:
                                                  l * HALF + (tt + 1) * TT],
                                           st=(l == 0), sp=(l == nic_p - 1):
                                           h.matmul(o[:], w, r, start=st, stop=sp))
                            kb.emit("te", fns,
                                    reads=[bwd] + [bheb[l] for l in range(nic_p)],
                                    writes=[bps])
                            dst = xs(dc, H * HALF + tt * TT)
                            kb.emit("dve", (lambda h, o=dst, i=ps:
                                            h.tensor_tensor(o, o, i[:], ALU.add)),
                                    reads=[bx[dc], bps], writes=[bx[dc]])
            kb.barrier()
            sc.release(m)

    lay_a = dict(wq=wqa, wk=wka, wv=wva, wo=woa, causal=True, rope=False,
                 bias=False)
    lay_b = dict(wq=wqb, wk=wkb, wv=wvb, wo=wob, causal=False, rope=True,
                 bias=True)

    kb.barrier()
    emit_attn(lay_a)
    if phases != "attn_a":
        emit_ffn()
    if phases in ("all", "attn_a+ffn+attn_b"):
        emit_attn(lay_b)
    if phases == "all":
        emit_ffn()

    for i in range(8):
        dma("sync",
            y_out[i * 2:(i + 1) * 2].rearrange("dc p t -> p dc t"),
            x_fm[:, i * 2 * S:(i + 1) * 2 * S].rearrange("p (dc t) -> p dc t",
                                                         dc=2),
            reads=[bx[dc] for dc in range(2 * i, 2 * i + 2)],
            key=f"y{i % 2}")
    kb._wait("sync", "DS_y0", kb.cnt["DS_y0"])
    kb._wait("sync", "DS_y1", kb.cnt["DS_y1"])

    kb.finalize()
    sc.release(0)
    for cm in reversed(kb.semcm):
        cm.__exit__(None, None, None)
    return nc


_prog_cache = {}


def kernel(**inputs):
    phases = _PHASES
    if phases not in _prog_cache:
        _prog_cache[phases] = build_program(phases)
    nc = _prog_cache[phases]
    shared = _host_shared(inputs)
    in_maps = []
    for b in range(8):
        m = dict(shared)
        m.update(_host_layouts(inputs, b))
        in_maps.append(m)
    res = run_bass_kernel_spmd(nc, in_maps, core_ids=list(range(8)))
    outs = []
    for b in range(8):
        y = np.asarray(res.results[b]["y"])
        outs.append(y.reshape(D, S).T)
    return np.stack(outs).astype(np.float32)


# version 24 (final)
# speedup vs baseline: 1.0136x; 1.0136x over previous
import math
import numpy as np
import ml_dtypes
import concourse.bass as bass
import concourse.mybir as mybir
from concourse.bass_utils import run_bass_kernel_spmd

F32 = mybir.dt.float32
BF16 = mybir.dt.bfloat16
AF = mybir.ActivationFunctionType
ALU = mybir.AluOpType
AX = mybir.AxisListType
BF = ml_dtypes.bfloat16

D = 2048
S = 2048
NH = 16
DH = 128
NDC = 16
NIC = 11
NE = 8
TT = 512
HALF = 1024
SCALE = 1.0 / math.sqrt(DH)
EPS = 1e-5

ENGINES = ("te", "act", "dve", "sync", "gp")
ENG_SEM = {"te": "pe", "act": "sa", "dve": "sv", "gp": "po"}

_PHASES = "all"
_FFN_PARTS = ("skip", "router", "shared", "routed", "down")
_ROUTER_EARLY = False


class Buf:
    __slots__ = ("writers", "readers", "name")

    def __init__(self, name=""):
        self.writers = {}
        self.readers = {}
        self.name = name


class KB:
    def __init__(self, nc):
        self.nc = nc
        self.ops = {e: [] for e in ENGINES}
        self.cnt = {}
        self.waited = {e: {} for e in ENGINES}
        self.semcm = []
        self.semh = {}

    def sem(self, name):
        if name not in self.semh:
            cm = self.nc.semaphore(name)
            self.semh[name] = cm.__enter__()
            self.semcm.append(cm)
            self.cnt[name] = 0
        return name

    def _wait(self, eng, sem, val):
        if val <= 0:
            return
        if self.waited[eng].get(sem, 0) >= val:
            return
        self.waited[eng][sem] = val
        self.ops[eng].append(("w", sem, val))

    def emit(self, eng, fns, reads=(), writes=(), dma_key=None):
        if not isinstance(fns, (list, tuple)):
            fns = [fns]
        if dma_key is not None:
            return self._emit_dma(eng, fns, reads, writes, dma_key)
        sem = self.sem(ENG_SEM[eng])
        need = {}
        for b in reads:
            for s, v in b.writers.items():
                need[s] = max(need.get(s, 0), v)
        for b in writes:
            for s, v in b.writers.items():
                need[s] = max(need.get(s, 0), v)
            for s, v in b.readers.items():
                need[s] = max(need.get(s, 0), v)
        for s in sorted(need):
            self._wait(eng, s, need[s])
        if False:
            pass
        else:
            for i, fn in enumerate(fns):
                if i == len(fns) - 1:
                    self.cnt[sem] += 1
                    self.ops[eng].append(("o", fn, sem, 1))
                else:
                    self.ops[eng].append(("o", fn, None, 0))
        v = self.cnt[sem]
        ws = set(id(b) for b in writes)
        for b in writes:
            b.writers[sem] = max(b.writers.get(sem, 0), v)
        for b in reads:
            if id(b) not in ws:
                b.readers[sem] = max(b.readers.get(sem, 0), v)

    def emit_seq(self, eng, steps, writes=()):
        sem = self.sem(ENG_SEM[eng])
        war = {}
        for b in writes:
            for s2, v in b.writers.items():
                war[s2] = max(war.get(s2, 0), v)
            for s2, v in b.readers.items():
                war[s2] = max(war.get(s2, 0), v)
        for s2 in sorted(war):
            self._wait(eng, s2, war[s2])
        nall = len(steps)
        for i, (fn, reads) in enumerate(steps):
            need = {}
            for b in reads:
                for s2, v in b.writers.items():
                    need[s2] = max(need.get(s2, 0), v)
            for s2 in sorted(need):
                self._wait(eng, s2, need[s2])
            if i == nall - 1:
                self.cnt[sem] += 1
                self.ops[eng].append(("o", fn, sem, 1))
            else:
                self.ops[eng].append(("o", fn, None, 0))
        v = self.cnt[sem]
        for b in writes:
            b.writers[sem] = max(b.writers.get(sem, 0), v)
        for i, (fn, reads) in enumerate(steps):
            for b in reads:
                b.readers[sem] = max(b.readers.get(sem, 0), v)

    def _emit_dma(self, eng, fns, reads, writes, key):
        sem = self.sem(("DG_" if eng == "gp" else "DS_") + key)
        need = {}
        for b in reads:
            for s2, v in b.writers.items():
                need[s2] = max(need.get(s2, 0), v)
        for b in writes:
            for s2, v in b.writers.items():
                need[s2] = max(need.get(s2, 0), v)
            for s2, v in b.readers.items():
                need[s2] = max(need.get(s2, 0), v)
        for s2 in sorted(need):
            self._wait(eng, s2, need[s2])
        for fn in fns:
            self.cnt[sem] += 16
            self.ops[eng].append(("o", fn, sem, 16))
        v = self.cnt[sem]
        ws = set(id(b) for b in writes)
        for b in writes:
            b.writers[sem] = max(b.writers.get(sem, 0), v)
        for b in reads:
            if id(b) not in ws:
                b.readers[sem] = max(b.readers.get(sem, 0), v)

    def barrier(self):
        snap = dict(self.cnt)
        for e in ENGINES:
            if e in ("gp", "te"):
                continue
            for s in sorted(snap):
                if s.startswith("DG_"):
                    continue
                self._wait(e, s, snap[s])

    def finalize(self):
        semh = self.semh
        nc = self.nc
        blk = nc.Block()
        block = blk.__enter__()
        deco = {"te": block.tensor, "act": block.scalar, "dve": block.vector,
                "sync": block.sync, "gp": block.gpsimd}

        def run(eng):
            def body(h):
                for item in self.ops[eng]:
                    if item[0] == "w":
                        h.wait_ge(semh[item[1]], item[2])
                    else:
                        inst = item[1](h)
                        if item[2] is not None:
                            inst.then_inc(semh[item[2]], item[3])
            return body
        for e in ENGINES:
            deco[e](run(e))
        blk.__exit__(None, None, None)


class Scope:

    def __init__(self, nc):
        self.nc = nc
        self.stack = []
        self.n = 0

    def sbuf(self, name, shape, dt):
        self.n += 1
        cm = self.nc.sbuf_tensor(f"{name}_{self.n}", shape, dt)
        t = cm.__enter__()
        self.stack.append(cm)
        return t

    def psum(self, name, shape, dt):
        self.n += 1
        cm = self.nc.psum_tensor(f"{name}_{self.n}", shape, dt)
        t = cm.__enter__()
        self.stack.append(cm)
        return t

    def mark(self):
        return len(self.stack)

    def release(self, mark):
        while len(self.stack) > mark:
            cm = self.stack.pop()
            cm.__exit__(None, None, None)


def _host_layouts(inputs, b):
    g = {}
    x = np.asarray(inputs["L0_attn_a_x"])[b]
    g["x_in"] = np.ascontiguousarray(x.T.reshape(NDC, 128, S))
    return g


def _host_shared(inputs):
    g = {}

    def qkv_t(w):
        a = np.asarray(w).transpose(0, 2, 1).reshape(NH, NDC, 128, DH)
        return np.ascontiguousarray(a.transpose(0, 2, 1, 3)).astype(BF)

    def oproj_t(wt):
        a = np.asarray(wt).reshape(NH, DH, NDC, 128)
        return np.ascontiguousarray(a.transpose(2, 0, 1, 3)).astype(BF)

    g["wqa"] = qkv_t(inputs["L0_attn_a_W_Q"])
    g["wka"] = qkv_t(inputs["L0_attn_a_W_K"])
    g["wva"] = qkv_t(inputs["L0_attn_a_W_V"])
    g["woa"] = oproj_t(np.asarray(inputs["L0_attn_a_W_O"]).transpose(0, 2, 1))

    g["wqb"] = qkv_t(np.asarray(inputs["L0_attn_b_Wq"]).reshape(NH, 128, D))
    g["wkb"] = qkv_t(np.asarray(inputs["L0_attn_b_Wk"]).reshape(NH, 128, D))
    g["wvb"] = qkv_t(np.asarray(inputs["L0_attn_b_Wv"]).reshape(NH, 128, D))
    g["wob"] = oproj_t(np.asarray(inputs["L0_attn_b_Wo"]).T.reshape(NH, DH, D))

    bq = np.asarray(inputs["L0_attn_b_bq"]).reshape(NH, 128)
    bk = np.asarray(inputs["L0_attn_b_bk"]).reshape(NH, 128)
    bv = np.asarray(inputs["L0_attn_b_bv"]).reshape(NH, 128)
    g["bqkv"] = np.ascontiguousarray(
        np.concatenate([bq, bk, bv], axis=0).T)
    g["bo"] = np.ascontiguousarray(
        np.asarray(inputs["L0_attn_b_bo"]).reshape(NDC, 128).T)

    def gate_t(w):
        a = np.asarray(w).T.reshape(NDC, 128, NIC, 128)
        return np.ascontiguousarray(a.transpose(2, 1, 0, 3))

    def down_t(w):
        a = np.asarray(w).T.reshape(NIC, 128, NDC, 128)
        return np.ascontiguousarray(a.transpose(2, 1, 0, 3))

    g["sg_t"] = gate_t(inputs["L0_ffn_shared_gate"]).astype(BF)
    g["su_t"] = gate_t(inputs["L0_ffn_shared_up"]).astype(BF)
    g["sd_t"] = down_t(inputs["L0_ffn_shared_down"]).astype(BF)

    wg = np.asarray(inputs["L0_ffn_w_gate"]).astype(BF)
    wu = np.asarray(inputs["L0_ffn_w_up"]).astype(BF)
    wd = np.asarray(inputs["L0_ffn_w_down"]).astype(BF)
    g["wg_t"] = np.stack([gate_t(wg[e]) for e in range(NE)])
    g["wu_t"] = np.stack([gate_t(wu[e]) for e in range(NE)])
    g["wd_t"] = np.stack([down_t(wd[e]) for e in range(NE)])

    rw = np.asarray(inputs["L0_ffn_router_w"]).T.reshape(NDC, 128, NE)
    g["rw"] = np.ascontiguousarray(rw).astype(BF)

    g["c_ident_bf"] = np.eye(128, dtype=BF)
    g["c_ident_f32"] = np.eye(8, 128, dtype=np.float32)
    g["c_ones_rep"] = np.ones((128, 128), dtype=BF)
    g["c_mask_t"] = np.where(np.arange(128)[:, None] > np.arange(128)[None, :],
                             np.float32(-1e9), np.float32(0.0))
    sel = np.zeros((8, NE * 128), dtype=BF)
    for e in range(NE):
        sel[e, e * 128:(e + 1) * 128] = 1.0
    g["c_sel"] = sel
    inv = 1.0 / (10000.0 ** (np.arange(0, DH, 2, dtype=np.float64) / DH))
    t = np.arange(S, dtype=np.float64)
    fr = t[None, :] * np.concatenate([inv, inv])[:, None]
    g["c_cos"] = np.cos(fr).astype(np.float32)
    g["c_eps"] = np.full((128, 1), EPS, dtype=np.float32)
    sin = np.sin(fr)
    sin[64:128] *= -1.0
    g["c_sin"] = sin.astype(np.float32)
    return g


def build_program(phases="all"):
    nc = bass.Bass("TRN2", target_bir_lowering=False)
    sc = Scope(nc)
    kb = KB(nc)

    dt_in = {}

    def din(name, shape, dt):
        dt_in[name] = nc.dram_tensor(name, shape, dt, kind="ExternalInput").ap()
        return dt_in[name]

    x_in = din("x_in", [NDC, 128, S], F32)
    wqa = din("wqa", [NH, 128, NDC, DH], BF16)
    wka = din("wka", [NH, 128, NDC, DH], BF16)
    wva = din("wva", [NH, 128, NDC, DH], BF16)
    woa = din("woa", [NDC, NH, DH, 128], BF16)
    wqb = din("wqb", [NH, 128, NDC, DH], BF16)
    wkb = din("wkb", [NH, 128, NDC, DH], BF16)
    wvb = din("wvb", [NH, 128, NDC, DH], BF16)
    wob = din("wob", [NDC, NH, DH, 128], BF16)
    bqkv = din("bqkv", [128, 48], F32)
    bo_in = din("bo", [128, 16], F32)
    sg_t = din("sg_t", [NIC, 128, NDC, 128], BF16)
    su_t = din("su_t", [NIC, 128, NDC, 128], BF16)
    sd_t = din("sd_t", [NDC, 128, NIC, 128], BF16)
    wg_t = din("wg_t", [NE, NIC, 128, NDC, 128], BF16)
    wu_t = din("wu_t", [NE, NIC, 128, NDC, 128], BF16)
    wd_t = din("wd_t", [NE, NDC, 128, NIC, 128], BF16)
    rw_in = din("rw", [NDC, 128, NE], BF16)
    c_ident_bf = din("c_ident_bf", [128, 128], BF16)
    c_ident_f32 = din("c_ident_f32", [8, 128], F32)
    c_ones_rep = din("c_ones_rep", [128, 128], BF16)
    c_mask_t = din("c_mask_t", [128, 128], F32)
    c_sel = din("c_sel", [8, NE * 128], BF16)
    c_cos = din("c_cos", [128, S], F32)
    c_sin = din("c_sin", [128, S], F32)
    c_eps = din("c_eps", [128, 1], F32)

    y_out = nc.dram_tensor("y", [NDC, 128, S], F32, kind="ExternalOutput").ap()

    QS = nc.dram_tensor("QS", [NH, 128, S], BF16, kind="Internal").ap()
    KS = nc.dram_tensor("KS", [NH, 128, S], BF16, kind="Internal").ap()
    VS = nc.dram_tensor("VS", [NH, 16, 128, DH], BF16, kind="Internal").ap()
    OS = nc.dram_tensor("OS", [NH, 128, S], BF16, kind="Internal").ap()

    x_fm = sc.sbuf("x_fm", [128, NDC * S], F32)
    ident_bf = sc.sbuf("ident_bf", [128, 128], BF16)
    ones_rep = sc.sbuf("ones_rep", [128, 128], BF16)
    mask_t = sc.sbuf("mask_t", [128, 128], F32)
    rw_sb = sc.sbuf("rw_sb", [128, NDC * NE], BF16)
    bqkv_sb = sc.sbuf("bqkv_sb", [128, 48], F32)
    bo_sb = sc.sbuf("bo_sb", [128, 16], F32)
    eps_sb = sc.sbuf("eps_sb", [128, 1], F32)
    wpool = [sc.sbuf(f"wp{i}", [128, 2048], BF16) for i in range(4)]
    wdpool = [sc.sbuf(f"wdp{i}", [128, 6 * 128], BF16) for i in range(2)]
    ostp = [sc.sbuf(f"ostp{i}", [128, 512], BF16) for i in range(2)]
    w8 = sc.sbuf("r_w8", [128, 8], F32)
    t8a = sc.sbuf("r_t8a", [128, 8], F32)
    t8b = sc.sbuf("r_t8b", [128, 8], F32)
    c8 = sc.sbuf("r_c8", [128, 8], BF16)
    nm8 = sc.sbuf("r_nm8", [128, 1], F32)
    sm8 = sc.sbuf("r_sm8", [128, 1], F32)
    rc8 = sc.sbuf("r_rc8", [128, 1], F32)

    pbw = [sc.psum(f"pbw{i}", [128, 1024], F32) for i in range(3)]
    pb7 = sc.psum("pb7", [128, 512], F32)
    pbt = sc.psum("pbt", [128, 512], BF16)
    pb = [pbw[0][:, 0:512], pbw[0][:, 512:1024],
          pbw[1][:, 0:512], pbw[1][:, 512:1024],
          pbw[2][:, 0:512], pbw[2][:, 512:1024], pb7[:]]

    bx = [Buf(f"x{dc}") for dc in range(NDC)]
    bpbw = [Buf(f"pbw{i}") for i in range(3)]
    bpb = [Buf(f"pb{i}") for i in range(7)]
    bpbt = Buf("pbt")
    bconst = Buf("const")
    bwpool = [Buf(f"wp{i}") for i in range(4)]
    bwdpool = [Buf(f"wdp{i}") for i in range(2)]
    bostp = [Buf(f"ostp{i}") for i in range(2)]
    bQS = [Buf(f"QS{h}") for h in range(NH)]
    bKS = [Buf(f"KS{h}") for h in range(NH)]
    bVS = [Buf(f"VS{h}") for h in range(NH)]
    bOS = [Buf(f"OS{h}") for h in range(NH)]

    def dma(eng, out_ap, in_ap, reads=(), writes=(), key="misc"):
        kb.emit(eng, (lambda h, o=out_ap, i=in_ap: h.dma_start(out=o, in_=i)),
                reads=reads, writes=writes, dma_key=key)

    dma("sync", ident_bf[:], c_ident_bf, writes=[bconst], key="const")
    dma("sync", ones_rep[:], c_ones_rep, writes=[bconst], key="const")
    dma("sync", mask_t[:], c_mask_t, writes=[bconst], key="const")
    dma("gp", rw_sb[:].rearrange("p (dc e) -> p dc e", dc=NDC),
        rw_in.rearrange("dc p e -> p dc e"), writes=[bconst], key="const")
    dma("sync", bqkv_sb[:], bqkv, writes=[bconst], key="const")
    dma("sync", bo_sb[:], bo_in, writes=[bconst], key="const")
    dma("sync", eps_sb[:], c_eps, writes=[bconst], key="const")
    for i in range(8):
        dma("sync" if i % 2 == 0 else "gp",
            x_fm[:, i * 2 * S:(i + 1) * 2 * S].rearrange("p (dc t) -> p dc t", dc=2),
            x_in[i * 2:(i + 1) * 2].rearrange("dc p t -> p dc t"),
            writes=[bx[dc] for dc in range(2 * i, 2 * i + 2)],
            key=f"xin{i // 2 % 2}")

    def xs(dc, t0, n=TT):
        return x_fm[:, dc * S + t0: dc * S + t0 + n]

    def emit_ln(H, m_b, rstd_b, bm, brs, h_half, bh, lnsc, blnsc, xbq, bxbq,
               add_skip):
        T0 = H * HALF
        for tt in range(2):
            t0 = T0 + tt * TT
            psx, bsx = pb[0], bpb[0]
            psq, bsq = pb[1], bpb[1]
            for dc in range(NDC):
                xb_t, bxb_t = xbq[dc % 2], bxbq[dc % 2]
                sq_t, bsq_t = xbq[2 + dc % 2], bxbq[2 + dc % 2]
                src = xs(dc, t0)
                kb.emit("dve", (lambda h, o=xb_t, i=src: h.tensor_copy(o[:], i)),
                        reads=[bx[dc]], writes=[bxb_t])
                kb.emit("act", (lambda h, o=sq_t, i=src:
                                h.activation(o[:], i, AF.Square)),
                        reads=[bx[dc]], writes=[bsq_t])
                last = dc == NDC - 1
                kb.emit("te", (lambda h, o=psx, i=xb_t, st=(dc == 0), sp=last:
                               h.matmul(o[:], ones_rep[:], i[:], start=st, stop=sp)),
                        reads=[bxb_t, bconst], writes=[bsx])
                kb.emit("te", (lambda h, o=psq, i=sq_t, st=(dc == 0), sp=last:
                               h.matmul(o[:], ones_rep[:], i[:], start=st, stop=sp)),
                        reads=[bsq_t, bconst], writes=[bsq])
            mslice = m_b[:, 0:TT]
            rslice = rstd_b[:, 0:TT]
            kb.emit("dve", (lambda h, o=mslice, i=psx:
                            h.tensor_scalar(o, i[:], 1.0 / D, None, ALU.mult)),
                    reads=[bsx], writes=[bm])
            kb.emit("dve", (lambda h, o=lnsc[0], i=psq:
                            h.tensor_scalar(o[:], i[:], 1.0 / D, None, ALU.mult)),
                    reads=[bsq], writes=[blnsc[0]])
            kb.emit("dve", (lambda h, o=lnsc[1], a=mslice:
                            h.tensor_tensor(o[:], a, a, ALU.mult)),
                    reads=[bm], writes=[blnsc[1]])
            kb.emit("dve", (lambda h, o=lnsc[0], a=lnsc[0], b2=lnsc[1]:
                            h.tensor_tensor(o[:], a[:], b2[:], ALU.subtract)),
                    reads=[blnsc[0], blnsc[1]], writes=[blnsc[0]])
            kb.emit("act", (lambda h, o=lnsc[1], i=lnsc[0]:
                            h.activation(o[:], i[:], AF.Ln,
                                         bias=eps_sb[:, 0:1])),
                    reads=[blnsc[0], bconst], writes=[blnsc[1]])
            kb.emit("act", (lambda h, o=rslice, i=lnsc[1]:
                            h.activation(o, i[:], AF.Exp, scale=-0.5)),
                    reads=[blnsc[1]], writes=[brs])
            for dc in range(NDC):
                src = xs(dc, t0)
                hsl = h_half[:, dc * HALF + tt * TT: dc * HALF + (tt + 1) * TT]
                kb.emit("dve", (lambda h, o=lnsc[0], a=src, b2=mslice:
                                h.tensor_tensor(o[:], a, b2, ALU.subtract)),
                        reads=[bx[dc], bm], writes=[blnsc[0]])
                if not add_skip:
                    kb.emit("dve", (lambda h, o=hsl, a=lnsc[0], b2=rslice:
                                    h.tensor_tensor(o, a[:], b2, ALU.mult)),
                            reads=[blnsc[0], brs], writes=[bh[(dc, tt)]])
                else:
                    kb.emit("dve", (lambda h, o=lnsc[1], a=lnsc[0], b2=rslice:
                                    h.tensor_tensor(o[:], a[:], b2, ALU.mult)),
                            reads=[blnsc[0], brs], writes=[blnsc[1]])
                    kb.emit("act", (lambda h, o=hsl, i=lnsc[1]:
                                    h.activation(o, i[:], AF.Copy)),
                            reads=[blnsc[1]], writes=[bh[(dc, tt)]])
                    kb.emit("dve", (lambda h, o=src, a=src, b2=lnsc[1]:
                                    h.tensor_tensor(o, a, b2[:], ALU.add)),
                            reads=[bx[dc], blnsc[1]], writes=[bx[dc]])

    def emit_attn(lay):
        m = sc.mark()
        h_half = sc.sbuf("a_h", [128, NDC * HALF], BF16)
        m_b = sc.sbuf("a_m", [128, TT], F32)
        rstd_b = sc.sbuf("a_r", [128, TT], F32)
        lnsc = [sc.sbuf(f"a_sc{i}", [128, 512], F32) for i in range(2)]
        xbq = [sc.sbuf(f"a_xb{i}", [128, 512], BF16) for i in range(4)]
        stg = [sc.sbuf(f"a_st{i}", [128, 512], BF16) for i in range(2)]
        vstg = [sc.sbuf(f"a_vs{i}", [128, 512], BF16) for i in range(2)]
        if lay["rope"]:
            cosb = [sc.sbuf(f"a_cos{i}", [128, TT], F32) for i in range(2)]
            sinb = [sc.sbuf(f"a_sin{i}", [128, TT], F32) for i in range(2)]
        bm, brs = Buf("m"), Buf("r")
        bh = {(dc, tt): Buf(f"h{dc}_{tt}") for dc in range(NDC) for tt in range(2)}
        blnsc = [Buf(f"sc{i}") for i in range(2)]
        bxbq = [Buf(f"xb{i}") for i in range(4)]
        bstg = [Buf(f"st{i}") for i in range(2)]
        bvstg = [Buf(f"vs{i}") for i in range(2)]
        bcos = [Buf(f"cos{i}") for i in range(2)]
        bsin = [Buf(f"sin{i}") for i in range(2)]

        wi = 0
        si = 0
        vi = 0
        for H in range(2):
            T0 = H * HALF
            if lay["rope"]:
                for tt in range(2):
                    tx = T0 + tt * TT
                    dma("sync", cosb[tt][:], c_cos[:, tx:tx + TT],
                        writes=[bcos[tt]], key=f"cos{tt}")
                    dma("sync", sinb[tt][:], c_sin[:, tx:tx + TT],
                        writes=[bsin[tt]], key=f"sin{tt}")
            emit_ln(H, m_b, rstd_b, bm, brs, h_half, bh, lnsc, blnsc,
                    xbq, bxbq, add_skip=False)
            for hd in range(NH):
                for mi, (wdr, outdr, boutb) in enumerate(
                        [(lay["wq"], QS, bQS), (lay["wk"], KS, bKS),
                         (lay["wv"], VS, bVS)]):
                    w_t, bw = wpool[wi % 4], bwpool[wi % 4]
                    wkey = f"wp{wi % 4}"
                    wi += 1
                    dma("gp", w_t[:].rearrange("p (dc e) -> p dc e", dc=NDC),
                        wdr[hd], writes=[bw], key=wkey)
                    for tt in range(2):
                        t0 = T0 + tt * TT
                        ps, bps = pb[2 + (tt + 2 * mi) % 2], bpb[2 + (tt + 2 * mi) % 2]
                        steps = []
                        for dc in range(NDC):
                            steps.append((lambda h, o=ps,
                                          w=w_t[:, dc * DH:(dc + 1) * DH],
                                          r=h_half[:, dc * HALF + tt * TT:
                                                   dc * HALF + (tt + 1) * TT],
                                          st=(dc == 0), sp=(dc == NDC - 1):
                                          h.matmul(o[:], w, r, start=st, stop=sp),
                                          [bw, bh[(dc, tt)]]))
                        kb.emit_seq("te", steps, writes=[bps])
                        is_v = mi == 2
                        if lay["bias"]:
                            bias_ap = bqkv_sb[:, mi * 16 + hd: mi * 16 + hd + 1]
                        if not is_v:
                            if lay["rope"]:
                                qf, bqf = lnsc[0], blnsc[0]
                                kb.emit("act", (lambda h, o=qf, i=ps, ba=bias_ap:
                                                h.activation(o[:], i[:], AF.Identity,
                                                             bias=ba))
                                        if lay["bias"] else
                                        (lambda h, o=qf, i=ps:
                                         h.activation(o[:], i[:], AF.Copy)),
                                        reads=[bps, bconst], writes=[bqf])
                                t1, bt1 = lnsc[1], blnsc[1]
                                qc, bqc = qf, bqf
                                cs = cosb[tt][:, :]
                                sn = sinb[tt][:, :]
                                kb.emit("dve",
                                        [(lambda h, o=t1, a=qf, s2=sn:
                                          h.tensor_tensor(o[0:64, :], a[64:128, :],
                                                          s2[64:128, :], ALU.mult)),
                                         (lambda h, o=t1, a=qf, s2=sn:
                                          h.tensor_tensor(o[64:128, :], a[0:64, :],
                                                          s2[0:64, :], ALU.mult))],
                                        reads=[bqf, bsin[tt]], writes=[bt1])
                                kb.emit("dve", (lambda h, o=qc, a=qf, c=cs:
                                                h.tensor_tensor(o[:], a[:], c,
                                                                ALU.mult)),
                                        reads=[bqf, bcos[tt]], writes=[bqc])
                                st_t, bst = stg[si % 2], bstg[si % 2]
                                si += 1
                                kb.emit("dve", (lambda h, o=st_t, a=qc, b2=t1:
                                                h.tensor_tensor(o[:], a[:], b2[:],
                                                                ALU.add)),
                                        reads=[bqc, bt1], writes=[bst])
                            else:
                                st_t, bst = stg[si % 2], bstg[si % 2]
                                si += 1
                                kb.emit("act", (lambda h, o=st_t, i=ps:
                                                h.activation(o[:], i[:], AF.Copy)),
                                        reads=[bps], writes=[bst])
                            dma("sync", outdr[hd][:, t0:t0 + TT], st_t[:],
                                reads=[bst], writes=[boutb[hd]],
                                key=f"stg{(si - 1) % 2}")
                        else:
                            vf, bvf = vstg[vi % 2], bvstg[vi % 2]
                            kb.emit("act", (lambda h, o=vf, i=ps, ba=bias_ap:
                                            h.activation(o[:], i[:], AF.Identity,
                                                         bias=ba))
                                    if lay["bias"] else
                                    (lambda h, o=vf, i=ps:
                                     h.activation(o[:], i[:], AF.Copy)),
                                    reads=[bps, bconst], writes=[bvf])
                            fns = [(lambda h, o=pbt[:, s * 128:(s + 1) * 128],
                                    i=vf[:, s * 128:(s + 1) * 128]:
                                    h.transpose(o, i, ident_bf[:]))
                                   for s in range(4)]
                            kb.emit("te", fns, reads=[bvf, bconst], writes=[bpbt])
                            vt, bvt = vstg[(vi + 1) % 2], bvstg[(vi + 1) % 2]
                            vi += 2
                            kb.emit("act", (lambda h, o=vt, i=pbt:
                                            h.activation(o[:], i[:], AF.Copy)),
                                    reads=[bpbt], writes=[bvt])
                            kc0 = t0 // 128
                            dma("sync",
                                VS[hd, kc0:kc0 + 4].rearrange("s p e -> p s e"),
                                vt[:].rearrange("p (s e) -> p s e", s=4),
                                reads=[bvt], writes=[bVS[hd]],
                                key=f"vstg{(vi - 1) % 2}")
        kb.barrier()
        sc.release(m)

        m = sc.mark()
        qh = [sc.sbuf(f"b_q{i}", [128, S], BF16) for i in range(2)]
        kh = [sc.sbuf(f"b_k{i}", [128, S], BF16) for i in range(2)]
        vh = [sc.sbuf(f"b_v{i}", [128, 16 * DH], BF16) for i in range(2)]
        pbuf = sc.sbuf("b_p", [128, 16 * 512], BF16)
        rsc = [sc.sbuf(f"b_r{i}", [128, 512], F32) for i in range(2)]
        bqh = [Buf(f"q{i}") for i in range(2)]
        bkh = [Buf(f"k{i}") for i in range(2)]
        bvh = [Buf(f"v{i}") for i in range(2)]
        bpk = [Buf(f"p{kc}") for kc in range(16)]
        brsc = [Buf(f"r{i}") for i in range(2)]
        oi = 0
        for hd in range(NH):
            sl = hd % 2
            dma("sync", qh[sl][:], QS[hd], reads=[bQS[hd]], writes=[bqh[sl]],
                key=f"qh{sl}")
            dma("sync", kh[sl][:], KS[hd], reads=[bKS[hd]], writes=[bkh[sl]],
                key=f"kh{sl}")
            for half8 in range(2):
                dma("sync",
                    vh[sl][:, half8 * 8 * DH:(half8 + 1) * 8 * DH]
                    .rearrange("p (s e) -> p s e", s=8),
                    VS[hd, half8 * 8:(half8 + 1) * 8].rearrange("s p e -> p s e"),
                    reads=[bVS[hd]], writes=[bvh[sl]], key=f"vh{sl}")
            for qb in range(4):
                kcm = (qb + 1) * 4 if lay["causal"] else 16
                q_ap = qh[sl][:, qb * 512:(qb + 1) * 512]
                for kc2 in range(0, kcm, 2):
                    wide, bwide = pbw[kc2 % 4 // 2], bpbw[kc2 % 4 // 2]
                    for ki in range(2):
                        kc = kc2 + ki
                        ps = wide[:, ki * 512:(ki + 1) * 512]
                        kb.emit("te", (lambda h, o=ps,
                                       w=kh[sl][:, kc * 128:(kc + 1) * 128],
                                       r=q_ap:
                                       h.matmul(o, w, r, start=True, stop=True)),
                                reads=[bkh[sl], bqh[sl]], writes=[bwide])
                        if lay["causal"] and kc >= qb * 4:
                            jj = kc - qb * 4
                            kb.emit("dve", (lambda h, o=ps, jj=jj:
                                            h.tensor_tensor(
                                                o[:, jj * 128:(jj + 1) * 128],
                                                o[:, jj * 128:(jj + 1) * 128],
                                                mask_t[:], ALU.add)),
                                    reads=[bwide, bconst], writes=[bwide])
                    psl = pbuf[:, kc2 * 512:(kc2 + 2) * 512]
                    kb.emit("act", (lambda h, o=psl, i=wide:
                                    h.activation(o, i[:], AF.Exp, scale=SCALE)),
                            reads=[bwide], writes=[bpk[kc2], bpk[kc2 + 1]])
                    for ki in range(2):
                        kc = kc2 + ki
                        if lay["causal"] and kc >= qb * 4:
                            jj = kc - qb * 4
                            if jj > 0:
                                kb.emit("dve", (lambda h,
                                                o=pbuf[:, kc * 512:(kc + 1) * 512],
                                                jj=jj:
                                                h.memset(o[:, 0:jj * 128], 0.0)),
                                        reads=[], writes=[bpk[kc]])
                pssum, bsum = pb[4], bpb[4]
                psout, bout = pb[5], bpb[5]
                fns_s, fns_v = [], []
                for kc in range(kcm):
                    psl = pbuf[:, kc * 512:(kc + 1) * 512]
                    fns_s.append(lambda h, o=pssum, r=psl, st=(kc == 0),
                                 sp=(kc == kcm - 1):
                                 h.matmul(o[:], ones_rep[:], r, start=st, stop=sp))
                    fns_v.append(lambda h, o=psout,
                                 w=vh[sl][:, kc * 128:(kc + 1) * 128], r=psl,
                                 st=(kc == 0), sp=(kc == kcm - 1):
                                 h.matmul(o[:], w, r, start=st, stop=sp))
                kb.emit("te", fns_s, reads=[bconst] + [bpk[k] for k in range(kcm)],
                        writes=[bsum])
                kb.emit("te", fns_v, reads=[bvh[sl]] + [bpk[k] for k in range(kcm)],
                        writes=[bout])
                r1, br1 = rsc[1], brsc[1]
                kb.emit("dve", (lambda h, o=r1, i=pssum:
                                h.reciprocal(o[:], i[:])),
                        reads=[bsum], writes=[br1])
                o_t, bo_t = ostp[oi % 2], bostp[oi % 2]
                oi += 1
                kb.emit("dve", (lambda h, o=o_t, a=psout, b2=r1:
                                h.tensor_tensor(o[:], a[:], b2[:], ALU.mult)),
                        reads=[bout, br1], writes=[bo_t])
                dma("gp", OS[hd][:, qb * 512:(qb + 1) * 512], o_t[:],
                    reads=[bo_t], writes=[bOS[hd]], key=f"ost{(oi - 1) % 2}")
        kb.barrier()
        sc.release(m)

        m = sc.mark()
        og = sc.sbuf("c_og", [128, 8 * S], BF16)
        bog8 = [Buf(f"og{i}") for i in range(8)]
        for g in range(2):
            for i8 in range(8):
                hd = g * 8 + i8
                dma("sync", og[:, i8 * S:(i8 + 1) * S], OS[hd],
                    reads=[bOS[hd]], writes=[bog8[i8]], key=f"og{i8 % 2}")
            for dc in range(NDC):
                w_t, bw = wpool[dc % 4], bwpool[dc % 4]
                dma("gp", w_t[:, 0:1024].rearrange("p (hh d) -> p hh d", hh=8),
                    lay["wo"][dc, g * 8:(g + 1) * 8].rearrange("hh p d -> p hh d"),
                    writes=[bw], key=f"wp{dc % 4}")
                for tt in range(4):
                    ps, bps = pb[2 + tt % 2], bpb[2 + tt % 2]
                    steps = []
                    for i8 in range(8):
                        steps.append((lambda h, o=ps,
                                      w=w_t[:, i8 * 128:(i8 + 1) * 128],
                                      r=og[:, i8 * S + tt * TT:
                                           i8 * S + (tt + 1) * TT],
                                      st=(i8 == 0), sp=(i8 == 7):
                                      h.matmul(o[:], w, r, start=st, stop=sp),
                                      [bw, bog8[i8]]))
                    kb.emit_seq("te", steps, writes=[bps])
                    dst = xs(dc, tt * TT)
                    kb.emit("dve", (lambda h, o=dst, i=ps:
                                    h.tensor_tensor(o, o, i[:], ALU.add)),
                            reads=[bx[dc], bps], writes=[bx[dc]])
        if lay["bias"]:
            for dc in range(NDC):
                kb.emit("dve", (lambda h, o=x_fm[:, dc * S:(dc + 1) * S], dc=dc:
                                h.tensor_scalar(o, o, bo_sb[:, dc:dc + 1], None,
                                                ALU.add)),
                        reads=[bx[dc], bconst], writes=[bx[dc]])
        kb.barrier()
        sc.release(m)

    def emit_ffn():
        for H in range(2):
            m = sc.mark()
            h_half = sc.sbuf("f_h", [128, NDC * HALF], BF16)
            bh = {(dc, tt): Buf(f"fh{dc}_{tt}")
                  for dc in range(NDC) for tt in range(2)}
            comb_fm = sc.sbuf("f_cf", [8, HALF], BF16)
            sel_sb = sc.sbuf("f_sel", [8, NE * 128], BF16)
            ident_f32 = sc.sbuf("f_idf", [8, 128], F32)
            bsel = Buf("sel")
            dma("sync", sel_sb[:], c_sel, writes=[bsel], key="sel")
            dma("sync", ident_f32[:], c_ident_f32, writes=[bsel], key="sel")
            lg_s = sc.sbuf("f_lg", [8, 1024], F32)
            m0 = sc.mark()
            m_b = sc.sbuf("f_m", [128, TT], F32)
            rstd_b = sc.sbuf("f_r", [128, TT], F32)
            lnsc = [sc.sbuf(f"f_sc{i}", [128, 512], F32) for i in range(3)]
            xbq = [sc.sbuf(f"f_xb{i}", [128, 512], BF16) for i in range(4)]
            bm, brs = Buf("fm"), Buf("fr")
            blnsc = [Buf(f"fsc{i}") for i in range(3)]
            bxbq = [Buf(f"fxb{i}") for i in range(4)]
            emit_ln(H, m_b, rstd_b, bm, brs, h_half, bh, lnsc, blnsc, xbq, bxbq,
                    add_skip=("skip" in _FFN_PARTS))
            bcf, blg = Buf("cf"), Buf("lg")
            bw8, bt8a, bt8b, bc8, bnm8, bsm8, brc8 = (
                Buf("w8"), Buf("t8a"), Buf("t8b"), Buf("c8"), Buf("nm8"),
                Buf("sm8"), Buf("rc8"))
            for tt in (range(2) if "router" in _FFN_PARTS else ()):
                t0 = H * HALF + tt * TT
                ps, bps = pb[6], bpb[6]
                steps = []
                for dc in range(NDC):
                    steps.append((lambda h, o=ps, w=rw_sb[:, dc * NE:(dc + 1) * NE],
                                  r=h_half[:, dc * HALF + tt * TT:
                                           dc * HALF + (tt + 1) * TT],
                                  st=(dc == 0), sp=(dc == NDC - 1):
                                  h.matmul(o[0:8, :], w, r, start=st, stop=sp),
                                  [bconst, bh[(dc, tt)]]))
                kb.emit_seq("te", steps, writes=[bps])
                lsl = lg_s[0:8, tt * TT:(tt + 1) * TT]
                kb.emit("act", (lambda h, o=lsl, i=ps:
                                h.activation(o, i[0:8, :], AF.Copy)),
                        reads=[bps], writes=[blg])

            def emit_router_post(tt):
                for st4 in range(4):
                    ps2, bps2 = pb[5], bpb[5]
                    kb.emit("te", (lambda h, o=ps2,
                                   w=lg_s[0:8, tt * TT + st4 * 128:
                                          tt * TT + (st4 + 1) * 128]:
                                   h.matmul(o[0:128, 0:8], w, ident_f32[0:8, 0:8],
                                            start=True, stop=True)),
                            reads=[blg, bsel], writes=[bps2])
                    kb.emit("dve", (lambda h, o=nm8, i=ps2:
                                    h.reduce_max(o[:], i[0:128, 0:8], axis=AX.X)),
                            reads=[bps2], writes=[bnm8])
                    kb.emit("dve", (lambda h, o=nm8, i=nm8:
                                    h.tensor_scalar(o[:], i[:], -1.0, None,
                                                    ALU.mult)),
                            reads=[bnm8], writes=[bnm8])
                    kb.emit("act", (lambda h, o=w8, i=ps2:
                                    h.activation(o[:], i[0:128, 0:8], AF.Exp,
                                                 bias=nm8[:, 0:1],
                                                 accum_out=sm8[:])),
                            reads=[bps2, bnm8], writes=[bw8, bsm8])
                    kb.emit("dve", (lambda h, o=rc8, i=sm8:
                                    h.reciprocal(o[:], i[:])),
                            reads=[bsm8], writes=[brc8])
                    kb.emit("dve", (lambda h, o=w8, i=w8:
                                    h.tensor_scalar(o[:], i[:], rc8[:, 0:1], None,
                                                    ALU.mult)),
                            reads=[bw8, brc8], writes=[bw8])
                    kb.emit("dve", (lambda h, o=nm8, i=w8:
                                    h.reduce_max(o[:], i[:], axis=AX.X)),
                            reads=[bw8], writes=[bnm8])
                    kb.emit("dve", (lambda h, o=t8a, i=w8:
                                    h.tensor_scalar(o[:], i[:], nm8[:, 0:1], None,
                                                    ALU.is_ge)),
                            reads=[bw8, bnm8], writes=[bt8a])
                    kb.emit("dve", (lambda h, o=t8a, i=t8a:
                                    h.tensor_scalar(o[:], i[:], 1e9, None,
                                                    ALU.mult)),
                            reads=[bt8a], writes=[bt8a])
                    kb.emit("dve", (lambda h, o=t8b, a=w8, b2=t8a:
                                    h.tensor_tensor(o[:], a[:], b2[:],
                                                    ALU.subtract)),
                            reads=[bw8, bt8a], writes=[bt8b])
                    kb.emit("dve", (lambda h, o=nm8, i=t8b:
                                    h.reduce_max(o[:], i[:], axis=AX.X)),
                            reads=[bt8b], writes=[bnm8])
                    kb.emit("dve", (lambda h, o=t8a, i=w8:
                                    h.tensor_scalar(o[:], i[:], nm8[:, 0:1], None,
                                                    ALU.is_ge)),
                            reads=[bw8, bnm8], writes=[bt8a])
                    kb.emit("dve", (lambda h, o=c8, a=w8, b2=t8a:
                                    h.tensor_tensor(o[:], a[:], b2[:], ALU.mult)),
                            reads=[bw8, bt8a], writes=[bc8])
                    kb.emit("te", (lambda h, o=pb[4], w=c8:
                                   h.matmul(o[0:8, 0:128], w[:], ident_bf[:],
                                            start=True, stop=True)),
                            reads=[bc8, bconst], writes=[bpb[4]])
                    csl = comb_fm[0:8, tt * TT + st4 * 128: tt * TT + (st4 + 1) * 128]
                    kb.emit("act", (lambda h, o=csl, i=pb[4]:
                                    h.activation(o, i[0:8, 0:128], AF.Copy)),
                            reads=[bpb[4]], writes=[bcf])

            kb.barrier()
            sc.release(m0)
            heb = sc.sbuf("f_heb", [128, 6 * HALF], BF16)
            sgq = [sc.sbuf(f"f_sg{i}", [128, 512], BF16) for i in range(2)]
            cbq = [sc.sbuf(f"f_cb{i}", [128, 512], BF16) for i in range(2)]
            bheb = [Buf(f"hb{i}") for i in range(6)]
            bsgq = [Buf(f"sg{i}") for i in range(2)]
            bcbq = [Buf(f"cb{i}") for i in range(2)]
            gi = 0
            if _ROUTER_EARLY:
                elist = [-1] + ([0] if "shared" in _FFN_PARTS else []) + \
                    (list(range(1, 9)) if "routed" in _FFN_PARTS else [])
            else:
                elist = ([0] if "shared" in _FFN_PARTS else []) + [-1] + \
                    (list(range(1, 9)) if "routed" in _FFN_PARTS else [])
            for e in elist:
                if e == -1:
                    if "router" in _FFN_PARTS:
                        emit_router_post(0)
                        emit_router_post(1)
                    continue
                shared = e == 0
                if not shared:
                    for tt in range(2):
                        cb, bcb = cbq[tt], bcbq[tt]
                        kb.emit("te", (lambda h, o=pb[6],
                                       w=sel_sb[0:8, (e - 1) * 128: e * 128],
                                       r=comb_fm[0:8, tt * TT:(tt + 1) * TT]:
                                       h.matmul(o[:], w, r, start=True, stop=True)),
                                reads=[bsel, bcf], writes=[bpb[6]])
                        kb.emit("act", (lambda h, o=cb, i=pb[6]:
                                        h.activation(o[:], i[:], AF.Copy)),
                                reads=[bpb[6]], writes=[bcb])
                for part, (ic0, ic1) in enumerate([(0, 6), (6, 11)]):
                    for ic in range(ic0, ic1):
                        wg_s, bwg = wpool[gi % 2], bwpool[gi % 2]
                        wu_s, bwu = wpool[2 + gi % 2], bwpool[2 + gi % 2]
                        gi += 1
                        wgkey, wukey = f"wp{(gi - 1) % 2}", f"wp{2 + (gi - 1) % 2}"
                        q = "gp" if shared else "sync"
                        gsrc = (sg_t[ic] if shared else wg_t[e - 1, ic])
                        usrc = (su_t[ic] if shared else wu_t[e - 1, ic])
                        dma(q, wg_s[:].rearrange("p (dc i) -> p dc i", dc=NDC),
                            gsrc, writes=[bwg], key=wgkey)
                        dma(q, wu_s[:].rearrange("p (dc i) -> p dc i", dc=NDC),
                            usrc, writes=[bwu], key=wukey)
                        for tt in range(2):
                            psg, bpsg = pb[2], bpb[2]
                            psu, bpsu = pb[3], bpb[3]
                            for ww, ps, bps, bw in ((wg_s, psg, bpsg, bwg),
                                                    (wu_s, psu, bpsu, bwu)):
                                steps = []
                                for dc in range(NDC):
                                    steps.append((lambda h, o=ps,
                                                  w=ww[:, dc * 128:(dc + 1) * 128],
                                                  r=h_half[:, dc * HALF + tt * TT:
                                                           dc * HALF + (tt + 1) * TT],
                                                  st=(dc == 0), sp=(dc == NDC - 1):
                                                  h.matmul(o[:], w, r,
                                                           start=st, stop=sp),
                                                  [bw, bh[(dc, tt)]]))
                                kb.emit_seq("te", steps, writes=[bps])
                            sg_s, bsg = sgq[tt], bsgq[tt]
                            kb.emit("act", (lambda h, o=sg_s, i=psg:
                                            h.activation(o[:], i[:], AF.Sigmoid)),
                                    reads=[bpsg], writes=[bsg])
                            hslot = ic - ic0
                            hsl = heb[:, hslot * HALF + tt * TT:
                                      hslot * HALF + (tt + 1) * TT]
                            if shared:
                                kb.emit("dve", (lambda h, o=hsl, a=sg_s, b2=psu:
                                                h.tensor_tensor(o, a[:], b2[:],
                                                                ALU.mult)),
                                        reads=[bsg, bpsu], writes=[bheb[hslot]])
                            else:
                                kb.emit("dve", (lambda h, o=hsl, a=sg_s, b2=psu:
                                                h.tensor_tensor(o, a[:], b2[:],
                                                                ALU.mult)),
                                        reads=[bsg, bpsu], writes=[bheb[hslot]])
                                kb.emit("dve", (lambda h, o=hsl, b2=cbq[tt]:
                                                h.tensor_tensor(o, o, b2[:],
                                                                ALU.mult)),
                                        reads=[bheb[hslot], bcbq[tt]],
                                        writes=[bheb[hslot]])
                    nic_p = ic1 - ic0
                    for dc in (range(NDC) if "down" in _FFN_PARTS else ()):
                        wd_s, bwd = wdpool[dc % 2], bwdpool[dc % 2]
                        src = (sd_t if shared else wd_t[e - 1])
                        ap = src[dc][:, ic0:ic1, :]
                        dma("gp" if shared else "sync",
                            wd_s[:, 0:nic_p * 128].rearrange(
                                "p (ic d) -> p ic d", ic=nic_p),
                            ap, writes=[bwd], key=f"wdp{dc % 2}")
                        for tt in range(2):
                            ps, bps = pb[4 + tt % 2], bpb[4 + tt % 2]
                            fns = []
                            for l in range(nic_p):
                                fns.append(lambda h, o=ps,
                                           w=wd_s[:, l * 128:(l + 1) * 128],
                                           r=heb[:, l * HALF + tt * TT:
                                                  l * HALF + (tt + 1) * TT],
                                           st=(l == 0), sp=(l == nic_p - 1):
                                           h.matmul(o[:], w, r, start=st, stop=sp))
                            kb.emit("te", fns,
                                    reads=[bwd] + [bheb[l] for l in range(nic_p)],
                                    writes=[bps])
                            dst = xs(dc, H * HALF + tt * TT)
                            kb.emit("dve", (lambda h, o=dst, i=ps:
                                            h.tensor_tensor(o, o, i[:], ALU.add)),
                                    reads=[bx[dc], bps], writes=[bx[dc]])
            kb.barrier()
            sc.release(m)

    lay_a = dict(wq=wqa, wk=wka, wv=wva, wo=woa, causal=True, rope=False,
                 bias=False)
    lay_b = dict(wq=wqb, wk=wkb, wv=wvb, wo=wob, causal=False, rope=True,
                 bias=True)

    kb.barrier()
    emit_attn(lay_a)
    if phases != "attn_a":
        emit_ffn()
    if phases in ("all", "attn_a+ffn+attn_b"):
        emit_attn(lay_b)
    if phases == "all":
        emit_ffn()

    for i in range(8):
        dma("sync",
            y_out[i * 2:(i + 1) * 2].rearrange("dc p t -> p dc t"),
            x_fm[:, i * 2 * S:(i + 1) * 2 * S].rearrange("p (dc t) -> p dc t",
                                                         dc=2),
            reads=[bx[dc] for dc in range(2 * i, 2 * i + 2)],
            key=f"y{i % 2}")
    kb._wait("sync", "DS_y0", kb.cnt["DS_y0"])
    kb._wait("sync", "DS_y1", kb.cnt["DS_y1"])

    kb.finalize()
    sc.release(0)
    for cm in reversed(kb.semcm):
        cm.__exit__(None, None, None)
    return nc


_prog_cache = {}


def kernel(**inputs):
    phases = _PHASES
    if phases not in _prog_cache:
        _prog_cache[phases] = build_program(phases)
    nc = _prog_cache[phases]
    shared = _host_shared(inputs)
    in_maps = []
    for b in range(8):
        m = dict(shared)
        m.update(_host_layouts(inputs, b))
        in_maps.append(m)
    res = run_bass_kernel_spmd(nc, in_maps, core_ids=list(range(8)))
    outs = []
    for b in range(8):
        y = np.asarray(res.results[b]["y"])
        outs.append(y.reshape(D, S).T)
    return np.stack(outs).astype(np.float32)


# version 25
# speedup vs baseline: 1.0846x; 1.0701x over previous
import math
import numpy as np
import ml_dtypes
import concourse.bass as bass
import concourse.mybir as mybir
from concourse.bass_utils import run_bass_kernel_spmd

F32 = mybir.dt.float32
BF16 = mybir.dt.bfloat16
AF = mybir.ActivationFunctionType
ALU = mybir.AluOpType
AX = mybir.AxisListType
BF = ml_dtypes.bfloat16

D = 2048
S = 2048
NH = 16
DH = 128
NDC = 16
NIC = 11
NE = 8
TT = 512
HALF = 1024
SCALE = 1.0 / math.sqrt(DH)
EPS = 1e-5

ENGINES = ("te", "act", "dve", "sync", "gp")
ENG_SEM = {"te": "pe", "act": "sa", "dve": "sv", "gp": "po"}

_PHASES = "all"
_FFN_PARTS = ("skip", "router", "shared", "routed", "down")
_ROUTER_EARLY = False


class Buf:
    __slots__ = ("writers", "readers", "name")

    def __init__(self, name=""):
        self.writers = {}
        self.readers = {}
        self.name = name


class KB:
    def __init__(self, nc):
        self.nc = nc
        self.ops = {e: [] for e in ENGINES}
        self.cnt = {}
        self.waited = {e: {} for e in ENGINES}
        self.semcm = []
        self.semh = {}

    def sem(self, name):
        if name not in self.semh:
            cm = self.nc.semaphore(name)
            self.semh[name] = cm.__enter__()
            self.semcm.append(cm)
            self.cnt[name] = 0
        return name

    def _wait(self, eng, sem, val):
        if val <= 0:
            return
        if self.waited[eng].get(sem, 0) >= val:
            return
        self.waited[eng][sem] = val
        self.ops[eng].append(("w", sem, val))

    def emit(self, eng, fns, reads=(), writes=(), dma_key=None):
        if not isinstance(fns, (list, tuple)):
            fns = [fns]
        if dma_key is not None:
            return self._emit_dma(eng, fns, reads, writes, dma_key)
        sem = self.sem(ENG_SEM[eng])
        need = {}
        for b in reads:
            for s, v in b.writers.items():
                need[s] = max(need.get(s, 0), v)
        for b in writes:
            for s, v in b.writers.items():
                need[s] = max(need.get(s, 0), v)
            for s, v in b.readers.items():
                need[s] = max(need.get(s, 0), v)
        for s in sorted(need):
            self._wait(eng, s, need[s])
        if False:
            pass
        else:
            for i, fn in enumerate(fns):
                if i == len(fns) - 1:
                    self.cnt[sem] += 1
                    self.ops[eng].append(("o", fn, sem, 1))
                else:
                    self.ops[eng].append(("o", fn, None, 0))
        v = self.cnt[sem]
        ws = set(id(b) for b in writes)
        for b in writes:
            b.writers[sem] = max(b.writers.get(sem, 0), v)
        for b in reads:
            if id(b) not in ws:
                b.readers[sem] = max(b.readers.get(sem, 0), v)

    def emit_seq(self, eng, steps, writes=()):
        sem = self.sem(ENG_SEM[eng])
        war = {}
        for b in writes:
            for s2, v in b.writers.items():
                war[s2] = max(war.get(s2, 0), v)
            for s2, v in b.readers.items():
                war[s2] = max(war.get(s2, 0), v)
        for s2 in sorted(war):
            self._wait(eng, s2, war[s2])
        nall = len(steps)
        for i, (fn, reads) in enumerate(steps):
            need = {}
            for b in reads:
                for s2, v in b.writers.items():
                    need[s2] = max(need.get(s2, 0), v)
            for s2 in sorted(need):
                self._wait(eng, s2, need[s2])
            if i == nall - 1:
                self.cnt[sem] += 1
                self.ops[eng].append(("o", fn, sem, 1))
            else:
                self.ops[eng].append(("o", fn, None, 0))
        v = self.cnt[sem]
        for b in writes:
            b.writers[sem] = max(b.writers.get(sem, 0), v)
        for i, (fn, reads) in enumerate(steps):
            for b in reads:
                b.readers[sem] = max(b.readers.get(sem, 0), v)

    def _emit_dma(self, eng, fns, reads, writes, key):
        sem = self.sem(("DG_" if eng == "gp" else "DS_") + key)
        need = {}
        for b in reads:
            for s2, v in b.writers.items():
                need[s2] = max(need.get(s2, 0), v)
        for b in writes:
            for s2, v in b.writers.items():
                need[s2] = max(need.get(s2, 0), v)
            for s2, v in b.readers.items():
                need[s2] = max(need.get(s2, 0), v)
        for s2 in sorted(need):
            self._wait(eng, s2, need[s2])
        for fn in fns:
            self.cnt[sem] += 16
            self.ops[eng].append(("o", fn, sem, 16))
        v = self.cnt[sem]
        ws = set(id(b) for b in writes)
        for b in writes:
            b.writers[sem] = max(b.writers.get(sem, 0), v)
        for b in reads:
            if id(b) not in ws:
                b.readers[sem] = max(b.readers.get(sem, 0), v)

    def barrier(self):
        snap = dict(self.cnt)
        for e in ENGINES:
            if e in ("gp", "te"):
                continue
            for s in sorted(snap):
                if s.startswith("DG_"):
                    continue
                self._wait(e, s, snap[s])

    def finalize(self):
        semh = self.semh
        nc = self.nc
        blk = nc.Block()
        block = blk.__enter__()
        deco = {"te": block.tensor, "act": block.scalar, "dve": block.vector,
                "sync": block.sync, "gp": block.gpsimd}

        def run(eng):
            def body(h):
                for item in self.ops[eng]:
                    if item[0] == "w":
                        h.wait_ge(semh[item[1]], item[2])
                    else:
                        inst = item[1](h)
                        if item[2] is not None:
                            inst.then_inc(semh[item[2]], item[3])
            return body
        for e in ENGINES:
            deco[e](run(e))
        blk.__exit__(None, None, None)


class Scope:

    def __init__(self, nc):
        self.nc = nc
        self.stack = []
        self.n = 0

    def sbuf(self, name, shape, dt):
        self.n += 1
        cm = self.nc.sbuf_tensor(f"{name}_{self.n}", shape, dt)
        t = cm.__enter__()
        self.stack.append(cm)
        return t

    def psum(self, name, shape, dt):
        self.n += 1
        cm = self.nc.psum_tensor(f"{name}_{self.n}", shape, dt)
        t = cm.__enter__()
        self.stack.append(cm)
        return t

    def mark(self):
        return len(self.stack)

    def release(self, mark):
        while len(self.stack) > mark:
            cm = self.stack.pop()
            cm.__exit__(None, None, None)


def _host_layouts(inputs, b):
    g = {}
    x = np.asarray(inputs["L0_attn_a_x"])[b]
    g["x_in"] = np.ascontiguousarray(x.T.reshape(NDC, 128, S))
    return g


def _host_shared(inputs):
    g = {}

    def qkv_t(w):
        a = np.asarray(w).transpose(0, 2, 1).reshape(NH, NDC, 128, DH)
        return np.ascontiguousarray(a.transpose(0, 2, 1, 3)).astype(BF)

    def oproj_t(wt):
        a = np.asarray(wt).reshape(NH, DH, NDC, 128)
        return np.ascontiguousarray(a.transpose(2, 0, 1, 3)).astype(BF)

    g["wqa"] = qkv_t(inputs["L0_attn_a_W_Q"])
    g["wka"] = qkv_t(inputs["L0_attn_a_W_K"])
    g["wva"] = qkv_t(inputs["L0_attn_a_W_V"])
    g["woa"] = oproj_t(np.asarray(inputs["L0_attn_a_W_O"]).transpose(0, 2, 1))

    g["wqb"] = qkv_t(np.asarray(inputs["L0_attn_b_Wq"]).reshape(NH, 128, D))
    g["wkb"] = qkv_t(np.asarray(inputs["L0_attn_b_Wk"]).reshape(NH, 128, D))
    g["wvb"] = qkv_t(np.asarray(inputs["L0_attn_b_Wv"]).reshape(NH, 128, D))
    g["wob"] = oproj_t(np.asarray(inputs["L0_attn_b_Wo"]).T.reshape(NH, DH, D))

    bq = np.asarray(inputs["L0_attn_b_bq"]).reshape(NH, 128)
    bk = np.asarray(inputs["L0_attn_b_bk"]).reshape(NH, 128)
    bv = np.asarray(inputs["L0_attn_b_bv"]).reshape(NH, 128)
    g["bqkv"] = np.ascontiguousarray(
        np.concatenate([bq, bk, bv], axis=0).T)
    g["bo"] = np.ascontiguousarray(
        np.asarray(inputs["L0_attn_b_bo"]).reshape(NDC, 128).T)

    def gate_t(w):
        a = np.asarray(w).T.reshape(NDC, 128, NIC, 128)
        return np.ascontiguousarray(a.transpose(2, 1, 0, 3))

    def down_t(w):
        a = np.asarray(w).T.reshape(NIC, 128, NDC, 128)
        return np.ascontiguousarray(a.transpose(2, 1, 0, 3))

    g["sg_t"] = gate_t(inputs["L0_ffn_shared_gate"]).astype(BF)
    g["su_t"] = gate_t(inputs["L0_ffn_shared_up"]).astype(BF)
    g["sd_t"] = down_t(inputs["L0_ffn_shared_down"]).astype(BF)

    wg = np.asarray(inputs["L0_ffn_w_gate"]).astype(BF)
    wu = np.asarray(inputs["L0_ffn_w_up"]).astype(BF)
    wd = np.asarray(inputs["L0_ffn_w_down"]).astype(BF)
    g["wg_t"] = np.stack([gate_t(wg[e]) for e in range(NE)])
    g["wu_t"] = np.stack([gate_t(wu[e]) for e in range(NE)])
    g["wd_t"] = np.stack([down_t(wd[e]) for e in range(NE)])

    rw = np.asarray(inputs["L0_ffn_router_w"]).T.reshape(NDC, 128, NE)
    g["rw"] = np.ascontiguousarray(rw).astype(BF)

    g["c_ident_bf"] = np.eye(128, dtype=BF)
    g["c_ident_f32"] = np.eye(8, 128, dtype=np.float32)
    g["c_ones_rep"] = np.ones((128, 128), dtype=BF)
    g["c_mask_t"] = np.where(np.arange(128)[:, None] > np.arange(128)[None, :],
                             np.float32(-1e9), np.float32(0.0))
    sel = np.zeros((8, NE * 128), dtype=BF)
    for e in range(NE):
        sel[e, e * 128:(e + 1) * 128] = 1.0
    g["c_sel"] = sel
    inv = 1.0 / (10000.0 ** (np.arange(0, DH, 2, dtype=np.float64) / DH))
    t = np.arange(S, dtype=np.float64)
    fr = t[None, :] * np.concatenate([inv, inv])[:, None]
    g["c_cos"] = np.cos(fr).astype(np.float32)
    g["c_eps"] = np.full((128, 1), EPS, dtype=np.float32)
    sin = np.sin(fr)
    sin[64:128] *= -1.0
    g["c_sin"] = sin.astype(np.float32)
    return g


def build_program(phases="all"):
    nc = bass.Bass("TRN2", target_bir_lowering=False)
    sc = Scope(nc)
    kb = KB(nc)

    dt_in = {}

    def din(name, shape, dt):
        dt_in[name] = nc.dram_tensor(name, shape, dt, kind="ExternalInput").ap()
        return dt_in[name]

    x_in = din("x_in", [NDC, 128, S], F32)
    wqa = din("wqa", [NH, 128, NDC, DH], BF16)
    wka = din("wka", [NH, 128, NDC, DH], BF16)
    wva = din("wva", [NH, 128, NDC, DH], BF16)
    woa = din("woa", [NDC, NH, DH, 128], BF16)
    wqb = din("wqb", [NH, 128, NDC, DH], BF16)
    wkb = din("wkb", [NH, 128, NDC, DH], BF16)
    wvb = din("wvb", [NH, 128, NDC, DH], BF16)
    wob = din("wob", [NDC, NH, DH, 128], BF16)
    bqkv = din("bqkv", [128, 48], F32)
    bo_in = din("bo", [128, 16], F32)
    sg_t = din("sg_t", [NIC, 128, NDC, 128], BF16)
    su_t = din("su_t", [NIC, 128, NDC, 128], BF16)
    sd_t = din("sd_t", [NDC, 128, NIC, 128], BF16)
    wg_t = din("wg_t", [NE, NIC, 128, NDC, 128], BF16)
    wu_t = din("wu_t", [NE, NIC, 128, NDC, 128], BF16)
    wd_t = din("wd_t", [NE, NDC, 128, NIC, 128], BF16)
    rw_in = din("rw", [NDC, 128, NE], BF16)
    c_ident_bf = din("c_ident_bf", [128, 128], BF16)
    c_ident_f32 = din("c_ident_f32", [8, 128], F32)
    c_ones_rep = din("c_ones_rep", [128, 128], BF16)
    c_mask_t = din("c_mask_t", [128, 128], F32)
    c_sel = din("c_sel", [8, NE * 128], BF16)
    c_cos = din("c_cos", [128, S], F32)
    c_sin = din("c_sin", [128, S], F32)
    c_eps = din("c_eps", [128, 1], F32)

    y_out = nc.dram_tensor("y", [NDC, 128, S], F32, kind="ExternalOutput").ap()

    QS = nc.dram_tensor("QS", [NH, 128, S], BF16, kind="Internal").ap()
    KS = nc.dram_tensor("KS", [NH, 128, S], BF16, kind="Internal").ap()
    VS = nc.dram_tensor("VS", [NH, 16, 128, DH], BF16, kind="Internal").ap()
    OS = nc.dram_tensor("OS", [NH, 128, S], BF16, kind="Internal").ap()

    x_fm = sc.sbuf("x_fm", [128, NDC * S], F32)
    ident_bf = sc.sbuf("ident_bf", [128, 128], BF16)
    ones_rep = sc.sbuf("ones_rep", [128, 128], BF16)
    mask_t = sc.sbuf("mask_t", [128, 128], F32)
    rw_sb = sc.sbuf("rw_sb", [128, NDC * NE], BF16)
    bqkv_sb = sc.sbuf("bqkv_sb", [128, 48], F32)
    bo_sb = sc.sbuf("bo_sb", [128, 16], F32)
    eps_sb = sc.sbuf("eps_sb", [128, 1], F32)
    wpool = [sc.sbuf(f"wp{i}", [128, 2048], BF16) for i in range(4)]
    wdpool = [sc.sbuf(f"wdp{i}", [128, 6 * 128], BF16) for i in range(2)]
    ostp = [sc.sbuf(f"ostp{i}", [128, 512], BF16) for i in range(2)]
    w8 = sc.sbuf("r_w8", [128, 8], F32)
    t8a = sc.sbuf("r_t8a", [128, 8], F32)
    t8b = sc.sbuf("r_t8b", [128, 8], F32)
    c8 = sc.sbuf("r_c8", [128, 8], BF16)
    nm8 = sc.sbuf("r_nm8", [128, 1], F32)
    sm8 = sc.sbuf("r_sm8", [128, 1], F32)
    rc8 = sc.sbuf("r_rc8", [128, 1], F32)

    pbw = [sc.psum(f"pbw{i}", [128, 1024], F32) for i in range(3)]
    pb7 = sc.psum("pb7", [128, 512], F32)
    pbt = sc.psum("pbt", [128, 512], BF16)
    pb = [pbw[0][:, 0:512], pbw[0][:, 512:1024],
          pbw[1][:, 0:512], pbw[1][:, 512:1024],
          pbw[2][:, 0:512], pbw[2][:, 512:1024], pb7[:]]

    bx = [Buf(f"x{dc}") for dc in range(NDC)]
    bpbw = [Buf(f"pbw{i}") for i in range(3)]
    bpb = [Buf(f"pb{i}") for i in range(7)]
    bpbt = Buf("pbt")
    bconst = Buf("const")
    bwpool = [Buf(f"wp{i}") for i in range(4)]
    bwdpool = [Buf(f"wdp{i}") for i in range(2)]
    bostp = [Buf(f"ostp{i}") for i in range(2)]
    bQS = [Buf(f"QS{h}") for h in range(NH)]
    bKS = [Buf(f"KS{h}") for h in range(NH)]
    bVS = [Buf(f"VS{h}") for h in range(NH)]
    bOS = [Buf(f"OS{h}") for h in range(NH)]

    def dma(eng, out_ap, in_ap, reads=(), writes=(), key="misc"):
        kb.emit(eng, (lambda h, o=out_ap, i=in_ap: h.dma_start(out=o, in_=i)),
                reads=reads, writes=writes, dma_key=key)

    dma("sync", ident_bf[:], c_ident_bf, writes=[bconst], key="const")
    dma("sync", ones_rep[:], c_ones_rep, writes=[bconst], key="const")
    dma("sync", mask_t[:], c_mask_t, writes=[bconst], key="const")
    dma("gp", rw_sb[:].rearrange("p (dc e) -> p dc e", dc=NDC),
        rw_in.rearrange("dc p e -> p dc e"), writes=[bconst], key="const")
    dma("sync", bqkv_sb[:], bqkv, writes=[bconst], key="const")
    dma("sync", bo_sb[:], bo_in, writes=[bconst], key="const")
    dma("sync", eps_sb[:], c_eps, writes=[bconst], key="const")
    for i in range(8):
        dma("sync" if i % 2 == 0 else "gp",
            x_fm[:, i * 2 * S:(i + 1) * 2 * S].rearrange("p (dc t) -> p dc t", dc=2),
            x_in[i * 2:(i + 1) * 2].rearrange("dc p t -> p dc t"),
            writes=[bx[dc] for dc in range(2 * i, 2 * i + 2)],
            key=f"xin{i // 2 % 2}")

    def xs(dc, t0, n=TT):
        return x_fm[:, dc * S + t0: dc * S + t0 + n]

    def emit_ln(H, m_b, rstd_b, bm, brs, h_half, bh, lnsc, blnsc, xbq, bxbq,
               add_skip):
        T0 = H * HALF
        for tt in range(2):
            t0 = T0 + tt * TT
            psx, bsx = pb[0], bpb[0]
            psq, bsq = pb[1], bpb[1]
            for dc in range(NDC):
                xb_t, bxb_t = xbq[dc % 2], bxbq[dc % 2]
                sq_t, bsq_t = xbq[2 + dc % 2], bxbq[2 + dc % 2]
                src = xs(dc, t0)
                kb.emit("dve", (lambda h, o=xb_t, i=src: h.tensor_copy(o[:], i)),
                        reads=[bx[dc]], writes=[bxb_t])
                kb.emit("act", (lambda h, o=sq_t, i=src:
                                h.activation(o[:], i, AF.Square)),
                        reads=[bx[dc]], writes=[bsq_t])
                last = dc == NDC - 1
                kb.emit("te", (lambda h, o=psx, i=xb_t, st=(dc == 0), sp=last:
                               h.matmul(o[:], ones_rep[:], i[:], start=st, stop=sp)),
                        reads=[bxb_t, bconst], writes=[bsx])
                kb.emit("te", (lambda h, o=psq, i=sq_t, st=(dc == 0), sp=last:
                               h.matmul(o[:], ones_rep[:], i[:], start=st, stop=sp)),
                        reads=[bsq_t, bconst], writes=[bsq])
            mslice = m_b[:, 0:TT]
            rslice = rstd_b[:, 0:TT]
            kb.emit("dve", (lambda h, o=mslice, i=psx:
                            h.tensor_scalar(o, i[:], 1.0 / D, None, ALU.mult)),
                    reads=[bsx], writes=[bm])
            kb.emit("dve", (lambda h, o=lnsc[0], i=psq:
                            h.tensor_scalar(o[:], i[:], 1.0 / D, None, ALU.mult)),
                    reads=[bsq], writes=[blnsc[0]])
            kb.emit("dve", (lambda h, o=lnsc[1], a=mslice:
                            h.tensor_tensor(o[:], a, a, ALU.mult)),
                    reads=[bm], writes=[blnsc[1]])
            kb.emit("dve", (lambda h, o=lnsc[0], a=lnsc[0], b2=lnsc[1]:
                            h.tensor_tensor(o[:], a[:], b2[:], ALU.subtract)),
                    reads=[blnsc[0], blnsc[1]], writes=[blnsc[0]])
            kb.emit("act", (lambda h, o=lnsc[1], i=lnsc[0]:
                            h.activation(o[:], i[:], AF.Ln,
                                         bias=eps_sb[:, 0:1])),
                    reads=[blnsc[0], bconst], writes=[blnsc[1]])
            kb.emit("act", (lambda h, o=rslice, i=lnsc[1]:
                            h.activation(o, i[:], AF.Exp, scale=-0.5)),
                    reads=[blnsc[1]], writes=[brs])
            for dc in range(NDC):
                src = xs(dc, t0)
                hsl = h_half[:, dc * HALF + tt * TT: dc * HALF + (tt + 1) * TT]
                kb.emit("dve", (lambda h, o=lnsc[0], a=src, b2=mslice:
                                h.tensor_tensor(o[:], a, b2, ALU.subtract)),
                        reads=[bx[dc], bm], writes=[blnsc[0]])
                if not add_skip:
                    kb.emit("dve", (lambda h, o=hsl, a=lnsc[0], b2=rslice:
                                    h.tensor_tensor(o, a[:], b2, ALU.mult)),
                            reads=[blnsc[0], brs], writes=[bh[(dc, tt)]])
                else:
                    kb.emit("dve", (lambda h, o=lnsc[1], a=lnsc[0], b2=rslice:
                                    h.tensor_tensor(o[:], a[:], b2, ALU.mult)),
                            reads=[blnsc[0], brs], writes=[blnsc[1]])
                    kb.emit("act", (lambda h, o=hsl, i=lnsc[1]:
                                    h.activation(o, i[:], AF.Copy)),
                            reads=[blnsc[1]], writes=[bh[(dc, tt)]])
                    kb.emit("dve", (lambda h, o=src, a=src, b2=lnsc[1]:
                                    h.tensor_tensor(o, a, b2[:], ALU.add)),
                            reads=[bx[dc], blnsc[1]], writes=[bx[dc]])

    def emit_attn(lay):
        m = sc.mark()
        h_half = sc.sbuf("a_h", [128, NDC * HALF], BF16)
        m_b = sc.sbuf("a_m", [128, TT], F32)
        rstd_b = sc.sbuf("a_r", [128, TT], F32)
        lnsc = [sc.sbuf(f"a_sc{i}", [128, 512], F32) for i in range(2)]
        xbq = [sc.sbuf(f"a_xb{i}", [128, 512], BF16) for i in range(4)]
        stg = [sc.sbuf(f"a_st{i}", [128, 512], BF16) for i in range(2)]
        vstg = [sc.sbuf(f"a_vs{i}", [128, 512], BF16) for i in range(2)]
        if lay["rope"]:
            cosb = [sc.sbuf(f"a_cos{i}", [128, TT], F32) for i in range(2)]
            sinb = [sc.sbuf(f"a_sin{i}", [128, TT], F32) for i in range(2)]
        bm, brs = Buf("m"), Buf("r")
        bh = {(dc, tt): Buf(f"h{dc}_{tt}") for dc in range(NDC) for tt in range(2)}
        blnsc = [Buf(f"sc{i}") for i in range(2)]
        bxbq = [Buf(f"xb{i}") for i in range(4)]
        bstg = [Buf(f"st{i}") for i in range(2)]
        bvstg = [Buf(f"vs{i}") for i in range(2)]
        bcos = [Buf(f"cos{i}") for i in range(2)]
        bsin = [Buf(f"sin{i}") for i in range(2)]

        wi = 0
        si = 0
        vi = 0
        for H in range(2):
            T0 = H * HALF
            if lay["rope"]:
                for tt in range(2):
                    tx = T0 + tt * TT
                    dma("sync", cosb[tt][:], c_cos[:, tx:tx + TT],
                        writes=[bcos[tt]], key=f"cos{tt}")
                    dma("sync", sinb[tt][:], c_sin[:, tx:tx + TT],
                        writes=[bsin[tt]], key=f"sin{tt}")
            emit_ln(H, m_b, rstd_b, bm, brs, h_half, bh, lnsc, blnsc,
                    xbq, bxbq, add_skip=False)
            for hd in range(NH):
                for mi, (wdr, outdr, boutb) in enumerate(
                        [(lay["wq"], QS, bQS), (lay["wk"], KS, bKS),
                         (lay["wv"], VS, bVS)]):
                    w_t, bw = wpool[wi % 4], bwpool[wi % 4]
                    wkey = f"wp{wi % 4}"
                    wi += 1
                    dma("gp", w_t[:].rearrange("p (dc e) -> p dc e", dc=NDC),
                        wdr[hd], writes=[bw], key=wkey)
                    for tt in range(2):
                        t0 = T0 + tt * TT
                        ps, bps = pb[2 + (tt + 2 * mi) % 2], bpb[2 + (tt + 2 * mi) % 2]
                        steps = []
                        for dc in range(NDC):
                            steps.append((lambda h, o=ps,
                                          w=w_t[:, dc * DH:(dc + 1) * DH],
                                          r=h_half[:, dc * HALF + tt * TT:
                                                   dc * HALF + (tt + 1) * TT],
                                          st=(dc == 0), sp=(dc == NDC - 1):
                                          h.matmul(o[:], w, r, start=st, stop=sp),
                                          [bw, bh[(dc, tt)]]))
                        kb.emit_seq("te", steps, writes=[bps])
                        is_v = mi == 2
                        if lay["bias"]:
                            bias_ap = bqkv_sb[:, mi * 16 + hd: mi * 16 + hd + 1]
                        if not is_v:
                            if lay["rope"]:
                                qf, bqf = lnsc[0], blnsc[0]
                                kb.emit("act", (lambda h, o=qf, i=ps, ba=bias_ap:
                                                h.activation(o[:], i[:], AF.Identity,
                                                             bias=ba))
                                        if lay["bias"] else
                                        (lambda h, o=qf, i=ps:
                                         h.activation(o[:], i[:], AF.Copy)),
                                        reads=[bps, bconst], writes=[bqf])
                                t1, bt1 = lnsc[1], blnsc[1]
                                qc, bqc = qf, bqf
                                cs = cosb[tt][:, :]
                                sn = sinb[tt][:, :]
                                kb.emit("dve",
                                        [(lambda h, o=t1, a=qf, s2=sn:
                                          h.tensor_tensor(o[0:64, :], a[64:128, :],
                                                          s2[64:128, :], ALU.mult)),
                                         (lambda h, o=t1, a=qf, s2=sn:
                                          h.tensor_tensor(o[64:128, :], a[0:64, :],
                                                          s2[0:64, :], ALU.mult))],
                                        reads=[bqf, bsin[tt]], writes=[bt1])
                                kb.emit("dve", (lambda h, o=qc, a=qf, c=cs:
                                                h.tensor_tensor(o[:], a[:], c,
                                                                ALU.mult)),
                                        reads=[bqf, bcos[tt]], writes=[bqc])
                                st_t, bst = stg[si % 2], bstg[si % 2]
                                si += 1
                                kb.emit("dve", (lambda h, o=st_t, a=qc, b2=t1:
                                                h.tensor_tensor(o[:], a[:], b2[:],
                                                                ALU.add)),
                                        reads=[bqc, bt1], writes=[bst])
                            else:
                                st_t, bst = stg[si % 2], bstg[si % 2]
                                si += 1
                                kb.emit("dve", (lambda h, o=st_t, i=ps:
                                                h.tensor_copy(o[:], i[:])),
                                        reads=[bps], writes=[bst])
                            dma("sync", outdr[hd][:, t0:t0 + TT], st_t[:],
                                reads=[bst], writes=[boutb[hd]],
                                key=f"stg{(si - 1) % 2}")
                        else:
                            vf, bvf = vstg[vi % 2], bvstg[vi % 2]
                            if lay["bias"]:
                                kb.emit("act", (lambda h, o=vf, i=ps, ba=bias_ap:
                                                h.activation(o[:], i[:],
                                                             AF.Identity,
                                                             bias=ba)),
                                        reads=[bps, bconst], writes=[bvf])
                            else:
                                kb.emit("dve", (lambda h, o=vf, i=ps:
                                                h.tensor_copy(o[:], i[:])),
                                        reads=[bps], writes=[bvf])
                            fns = [(lambda h, o=pbt[:, s * 128:(s + 1) * 128],
                                    i=vf[:, s * 128:(s + 1) * 128]:
                                    h.transpose(o, i, ident_bf[:]))
                                   for s in range(4)]
                            kb.emit("te", fns, reads=[bvf, bconst], writes=[bpbt])
                            vt, bvt = vstg[(vi + 1) % 2], bvstg[(vi + 1) % 2]
                            vi += 2
                            if lay["rope"]:
                                kb.emit("act", (lambda h, o=vt, i=pbt:
                                                h.activation(o[:], i[:],
                                                             AF.Copy)),
                                        reads=[bpbt], writes=[bvt])
                            else:
                                kb.emit("dve", (lambda h, o=vt, i=pbt:
                                                h.tensor_copy(o[:], i[:])),
                                        reads=[bpbt], writes=[bvt])
                            kc0 = t0 // 128
                            dma("sync",
                                VS[hd, kc0:kc0 + 4].rearrange("s p e -> p s e"),
                                vt[:].rearrange("p (s e) -> p s e", s=4),
                                reads=[bvt], writes=[bVS[hd]],
                                key=f"vstg{(vi - 1) % 2}")
        kb.barrier()
        sc.release(m)

        m = sc.mark()
        qh = [sc.sbuf(f"b_q{i}", [128, S], BF16) for i in range(2)]
        kh = [sc.sbuf(f"b_k{i}", [128, S], BF16) for i in range(2)]
        vh = [sc.sbuf(f"b_v{i}", [128, 16 * DH], BF16) for i in range(2)]
        pbuf = sc.sbuf("b_p", [128, 16 * 512], BF16)
        rsc = [sc.sbuf(f"b_r{i}", [128, 512], F32) for i in range(2)]
        bqh = [Buf(f"q{i}") for i in range(2)]
        bkh = [Buf(f"k{i}") for i in range(2)]
        bvh = [Buf(f"v{i}") for i in range(2)]
        bpk = [Buf(f"p{kc}") for kc in range(16)]
        brsc = [Buf(f"r{i}") for i in range(2)]
        oi = 0
        for hd in range(NH):
            sl = hd % 2
            dma("sync", qh[sl][:], QS[hd], reads=[bQS[hd]], writes=[bqh[sl]],
                key=f"qh{sl}")
            dma("sync", kh[sl][:], KS[hd], reads=[bKS[hd]], writes=[bkh[sl]],
                key=f"kh{sl}")
            for half8 in range(2):
                dma("sync",
                    vh[sl][:, half8 * 8 * DH:(half8 + 1) * 8 * DH]
                    .rearrange("p (s e) -> p s e", s=8),
                    VS[hd, half8 * 8:(half8 + 1) * 8].rearrange("s p e -> p s e"),
                    reads=[bVS[hd]], writes=[bvh[sl]], key=f"vh{sl}")
            for qb in range(4):
                kcm = (qb + 1) * 4 if lay["causal"] else 16
                q_ap = qh[sl][:, qb * 512:(qb + 1) * 512]
                for kc2 in range(0, kcm, 2):
                    wide, bwide = pbw[kc2 % 4 // 2], bpbw[kc2 % 4 // 2]
                    for ki in range(2):
                        kc = kc2 + ki
                        ps = wide[:, ki * 512:(ki + 1) * 512]
                        kb.emit("te", (lambda h, o=ps,
                                       w=kh[sl][:, kc * 128:(kc + 1) * 128],
                                       r=q_ap:
                                       h.matmul(o, w, r, start=True, stop=True)),
                                reads=[bkh[sl], bqh[sl]], writes=[bwide])
                        if lay["causal"] and kc >= qb * 4:
                            jj = kc - qb * 4
                            kb.emit("dve", (lambda h, o=ps, jj=jj:
                                            h.tensor_tensor(
                                                o[:, jj * 128:(jj + 1) * 128],
                                                o[:, jj * 128:(jj + 1) * 128],
                                                mask_t[:], ALU.add)),
                                    reads=[bwide, bconst], writes=[bwide])
                    psl = pbuf[:, kc2 * 512:(kc2 + 2) * 512]
                    kb.emit("act", (lambda h, o=psl, i=wide:
                                    h.activation(o, i[:], AF.Exp, scale=SCALE)),
                            reads=[bwide], writes=[bpk[kc2], bpk[kc2 + 1]])
                    for ki in range(2):
                        kc = kc2 + ki
                        if lay["causal"] and kc >= qb * 4:
                            jj = kc - qb * 4
                            if jj > 0:
                                kb.emit("dve", (lambda h,
                                                o=pbuf[:, kc * 512:(kc + 1) * 512],
                                                jj=jj:
                                                h.memset(o[:, 0:jj * 128], 0.0)),
                                        reads=[], writes=[bpk[kc]])
                pssum, bsum = pb[4], bpb[4]
                psout, bout = pb[5], bpb[5]
                fns_s, fns_v = [], []
                for kc in range(kcm):
                    psl = pbuf[:, kc * 512:(kc + 1) * 512]
                    fns_s.append(lambda h, o=pssum, r=psl, st=(kc == 0),
                                 sp=(kc == kcm - 1):
                                 h.matmul(o[:], ones_rep[:], r, start=st, stop=sp))
                    fns_v.append(lambda h, o=psout,
                                 w=vh[sl][:, kc * 128:(kc + 1) * 128], r=psl,
                                 st=(kc == 0), sp=(kc == kcm - 1):
                                 h.matmul(o[:], w, r, start=st, stop=sp))
                kb.emit("te", fns_s, reads=[bconst] + [bpk[k] for k in range(kcm)],
                        writes=[bsum])
                kb.emit("te", fns_v, reads=[bvh[sl]] + [bpk[k] for k in range(kcm)],
                        writes=[bout])
                r1, br1 = rsc[1], brsc[1]
                kb.emit("dve", (lambda h, o=r1, i=pssum:
                                h.reciprocal(o[:], i[:])),
                        reads=[bsum], writes=[br1])
                o_t, bo_t = ostp[oi % 2], bostp[oi % 2]
                oi += 1
                kb.emit("dve", (lambda h, o=o_t, a=psout, b2=r1:
                                h.tensor_tensor(o[:], a[:], b2[:], ALU.mult)),
                        reads=[bout, br1], writes=[bo_t])
                dma("gp", OS[hd][:, qb * 512:(qb + 1) * 512], o_t[:],
                    reads=[bo_t], writes=[bOS[hd]], key=f"ost{(oi - 1) % 2}")
        kb.barrier()
        sc.release(m)

        m = sc.mark()
        og = sc.sbuf("c_og", [128, 8 * S], BF16)
        bog8 = [Buf(f"og{i}") for i in range(8)]
        for g in range(2):
            for i8 in range(8):
                hd = g * 8 + i8
                dma("sync", og[:, i8 * S:(i8 + 1) * S], OS[hd],
                    reads=[bOS[hd]], writes=[bog8[i8]], key=f"og{i8 % 2}")
            for dc in range(NDC):
                w_t, bw = wpool[dc % 4], bwpool[dc % 4]
                dma("gp", w_t[:, 0:1024].rearrange("p (hh d) -> p hh d", hh=8),
                    lay["wo"][dc, g * 8:(g + 1) * 8].rearrange("hh p d -> p hh d"),
                    writes=[bw], key=f"wp{dc % 4}")
                for tt in range(4):
                    ps, bps = pb[2 + tt % 2], bpb[2 + tt % 2]
                    steps = []
                    for i8 in range(8):
                        steps.append((lambda h, o=ps,
                                      w=w_t[:, i8 * 128:(i8 + 1) * 128],
                                      r=og[:, i8 * S + tt * TT:
                                           i8 * S + (tt + 1) * TT],
                                      st=(i8 == 0), sp=(i8 == 7):
                                      h.matmul(o[:], w, r, start=st, stop=sp),
                                      [bw, bog8[i8]]))
                    kb.emit_seq("te", steps, writes=[bps])
                    dst = xs(dc, tt * TT)
                    kb.emit("dve", (lambda h, o=dst, i=ps:
                                    h.tensor_tensor(o, o, i[:], ALU.add)),
                            reads=[bx[dc], bps], writes=[bx[dc]])
        if lay["bias"]:
            for dc in range(NDC):
                kb.emit("dve", (lambda h, o=x_fm[:, dc * S:(dc + 1) * S], dc=dc:
                                h.tensor_scalar(o, o, bo_sb[:, dc:dc + 1], None,
                                                ALU.add)),
                        reads=[bx[dc], bconst], writes=[bx[dc]])
        kb.barrier()
        sc.release(m)

    def emit_ffn():
        for H in range(2):
            m = sc.mark()
            h_half = sc.sbuf("f_h", [128, NDC * HALF], BF16)
            bh = {(dc, tt): Buf(f"fh{dc}_{tt}")
                  for dc in range(NDC) for tt in range(2)}
            comb_fm = sc.sbuf("f_cf", [8, HALF], BF16)
            sel_sb = sc.sbuf("f_sel", [8, NE * 128], BF16)
            ident_f32 = sc.sbuf("f_idf", [8, 128], F32)
            bsel = Buf("sel")
            dma("sync", sel_sb[:], c_sel, writes=[bsel], key="sel")
            dma("sync", ident_f32[:], c_ident_f32, writes=[bsel], key="sel")
            lg_s = sc.sbuf("f_lg", [8, 1024], F32)
            m0 = sc.mark()
            m_b = sc.sbuf("f_m", [128, TT], F32)
            rstd_b = sc.sbuf("f_r", [128, TT], F32)
            lnsc = [sc.sbuf(f"f_sc{i}", [128, 512], F32) for i in range(3)]
            xbq = [sc.sbuf(f"f_xb{i}", [128, 512], BF16) for i in range(4)]
            bm, brs = Buf("fm"), Buf("fr")
            blnsc = [Buf(f"fsc{i}") for i in range(3)]
            bxbq = [Buf(f"fxb{i}") for i in range(4)]
            emit_ln(H, m_b, rstd_b, bm, brs, h_half, bh, lnsc, blnsc, xbq, bxbq,
                    add_skip=("skip" in _FFN_PARTS))
            bcf, blg = Buf("cf"), Buf("lg")
            bw8, bt8a, bt8b, bc8, bnm8, bsm8, brc8 = (
                Buf("w8"), Buf("t8a"), Buf("t8b"), Buf("c8"), Buf("nm8"),
                Buf("sm8"), Buf("rc8"))
            for tt in (range(2) if "router" in _FFN_PARTS else ()):
                t0 = H * HALF + tt * TT
                ps, bps = pb[6], bpb[6]
                steps = []
                for dc in range(NDC):
                    steps.append((lambda h, o=ps, w=rw_sb[:, dc * NE:(dc + 1) * NE],
                                  r=h_half[:, dc * HALF + tt * TT:
                                           dc * HALF + (tt + 1) * TT],
                                  st=(dc == 0), sp=(dc == NDC - 1):
                                  h.matmul(o[0:8, :], w, r, start=st, stop=sp),
                                  [bconst, bh[(dc, tt)]]))
                kb.emit_seq("te", steps, writes=[bps])
                lsl = lg_s[0:8, tt * TT:(tt + 1) * TT]
                kb.emit("act", (lambda h, o=lsl, i=ps:
                                h.activation(o, i[0:8, :], AF.Copy)),
                        reads=[bps], writes=[blg])

            def emit_router_post(tt):
                for st4 in range(4):
                    ps2, bps2 = pb[5], bpb[5]
                    kb.emit("te", (lambda h, o=ps2,
                                   w=lg_s[0:8, tt * TT + st4 * 128:
                                          tt * TT + (st4 + 1) * 128]:
                                   h.matmul(o[0:128, 0:8], w, ident_f32[0:8, 0:8],
                                            start=True, stop=True)),
                            reads=[blg, bsel], writes=[bps2])
                    kb.emit("dve", (lambda h, o=nm8, i=ps2:
                                    h.reduce_max(o[:], i[0:128, 0:8], axis=AX.X)),
                            reads=[bps2], writes=[bnm8])
                    kb.emit("dve", (lambda h, o=nm8, i=nm8:
                                    h.tensor_scalar(o[:], i[:], -1.0, None,
                                                    ALU.mult)),
                            reads=[bnm8], writes=[bnm8])
                    kb.emit("act", (lambda h, o=w8, i=ps2:
                                    h.activation(o[:], i[0:128, 0:8], AF.Exp,
                                                 bias=nm8[:, 0:1],
                                                 accum_out=sm8[:])),
                            reads=[bps2, bnm8], writes=[bw8, bsm8])
                    kb.emit("dve", (lambda h, o=rc8, i=sm8:
                                    h.reciprocal(o[:], i[:])),
                            reads=[bsm8], writes=[brc8])
                    kb.emit("dve", (lambda h, o=w8, i=w8:
                                    h.tensor_scalar(o[:], i[:], rc8[:, 0:1], None,
                                                    ALU.mult)),
                            reads=[bw8, brc8], writes=[bw8])
                    kb.emit("dve", (lambda h, o=nm8, i=w8:
                                    h.reduce_max(o[:], i[:], axis=AX.X)),
                            reads=[bw8], writes=[bnm8])
                    kb.emit("dve", (lambda h, o=t8a, i=w8:
                                    h.tensor_scalar(o[:], i[:], nm8[:, 0:1], None,
                                                    ALU.is_ge)),
                            reads=[bw8, bnm8], writes=[bt8a])
                    kb.emit("dve", (lambda h, o=t8a, i=t8a:
                                    h.tensor_scalar(o[:], i[:], 1e9, None,
                                                    ALU.mult)),
                            reads=[bt8a], writes=[bt8a])
                    kb.emit("dve", (lambda h, o=t8b, a=w8, b2=t8a:
                                    h.tensor_tensor(o[:], a[:], b2[:],
                                                    ALU.subtract)),
                            reads=[bw8, bt8a], writes=[bt8b])
                    kb.emit("dve", (lambda h, o=nm8, i=t8b:
                                    h.reduce_max(o[:], i[:], axis=AX.X)),
                            reads=[bt8b], writes=[bnm8])
                    kb.emit("dve", (lambda h, o=t8a, i=w8:
                                    h.tensor_scalar(o[:], i[:], nm8[:, 0:1], None,
                                                    ALU.is_ge)),
                            reads=[bw8, bnm8], writes=[bt8a])
                    kb.emit("dve", (lambda h, o=c8, a=w8, b2=t8a:
                                    h.tensor_tensor(o[:], a[:], b2[:], ALU.mult)),
                            reads=[bw8, bt8a], writes=[bc8])
                    kb.emit("te", (lambda h, o=pb[4], w=c8:
                                   h.matmul(o[0:8, 0:128], w[:], ident_bf[:],
                                            start=True, stop=True)),
                            reads=[bc8, bconst], writes=[bpb[4]])
                    csl = comb_fm[0:8, tt * TT + st4 * 128: tt * TT + (st4 + 1) * 128]
                    kb.emit("act", (lambda h, o=csl, i=pb[4]:
                                    h.activation(o, i[0:8, 0:128], AF.Copy)),
                            reads=[bpb[4]], writes=[bcf])

            kb.barrier()
            sc.release(m0)
            heb = sc.sbuf("f_heb", [128, 6 * HALF], BF16)
            sgq = [sc.sbuf(f"f_sg{i}", [128, 512], BF16) for i in range(2)]
            cbq = [sc.sbuf(f"f_cb{i}", [128, 512], BF16) for i in range(2)]
            bheb = [Buf(f"hb{i}") for i in range(6)]
            bsgq = [Buf(f"sg{i}") for i in range(2)]
            bcbq = [Buf(f"cb{i}") for i in range(2)]
            gi = 0
            if _ROUTER_EARLY:
                elist = [-1] + ([0] if "shared" in _FFN_PARTS else []) + \
                    (list(range(1, 9)) if "routed" in _FFN_PARTS else [])
            else:
                elist = ([0] if "shared" in _FFN_PARTS else []) + [-1] + \
                    (list(range(1, 9)) if "routed" in _FFN_PARTS else [])
            for e in elist:
                if e == -1:
                    if "router" in _FFN_PARTS:
                        emit_router_post(0)
                        emit_router_post(1)
                    continue
                shared = e == 0
                if not shared:
                    for tt in range(2):
                        cb, bcb = cbq[tt], bcbq[tt]
                        kb.emit("te", (lambda h, o=pb[6],
                                       w=sel_sb[0:8, (e - 1) * 128: e * 128],
                                       r=comb_fm[0:8, tt * TT:(tt + 1) * TT]:
                                       h.matmul(o[:], w, r, start=True, stop=True)),
                                reads=[bsel, bcf], writes=[bpb[6]])
                        kb.emit("act", (lambda h, o=cb, i=pb[6]:
                                        h.activation(o[:], i[:], AF.Copy)),
                                reads=[bpb[6]], writes=[bcb])
                for part, (ic0, ic1) in enumerate([(0, 6), (6, 11)]):
                    for ic in range(ic0, ic1):
                        wg_s, bwg = wpool[gi % 2], bwpool[gi % 2]
                        wu_s, bwu = wpool[2 + gi % 2], bwpool[2 + gi % 2]
                        gi += 1
                        wgkey, wukey = f"wp{(gi - 1) % 2}", f"wp{2 + (gi - 1) % 2}"
                        q = "gp" if shared else "sync"
                        gsrc = (sg_t[ic] if shared else wg_t[e - 1, ic])
                        usrc = (su_t[ic] if shared else wu_t[e - 1, ic])
                        dma(q, wg_s[:].rearrange("p (dc i) -> p dc i", dc=NDC),
                            gsrc, writes=[bwg], key=wgkey)
                        dma(q, wu_s[:].rearrange("p (dc i) -> p dc i", dc=NDC),
                            usrc, writes=[bwu], key=wukey)
                        for tt in range(2):
                            psg, bpsg = pb[2], bpb[2]
                            psu, bpsu = pb[3], bpb[3]
                            for ww, ps, bps, bw in ((wg_s, psg, bpsg, bwg),
                                                    (wu_s, psu, bpsu, bwu)):
                                steps = []
                                for dc in range(NDC):
                                    steps.append((lambda h, o=ps,
                                                  w=ww[:, dc * 128:(dc + 1) * 128],
                                                  r=h_half[:, dc * HALF + tt * TT:
                                                           dc * HALF + (tt + 1) * TT],
                                                  st=(dc == 0), sp=(dc == NDC - 1):
                                                  h.matmul(o[:], w, r,
                                                           start=st, stop=sp),
                                                  [bw, bh[(dc, tt)]]))
                                kb.emit_seq("te", steps, writes=[bps])
                            sg_s, bsg = sgq[tt], bsgq[tt]
                            kb.emit("act", (lambda h, o=sg_s, i=psg:
                                            h.activation(o[:], i[:], AF.Sigmoid)),
                                    reads=[bpsg], writes=[bsg])
                            hslot = ic - ic0
                            hsl = heb[:, hslot * HALF + tt * TT:
                                      hslot * HALF + (tt + 1) * TT]
                            if shared:
                                kb.emit("dve", (lambda h, o=hsl, a=sg_s, b2=psu:
                                                h.tensor_tensor(o, a[:], b2[:],
                                                                ALU.mult)),
                                        reads=[bsg, bpsu], writes=[bheb[hslot]])
                            else:
                                kb.emit("dve", (lambda h, o=hsl, a=sg_s, b2=psu:
                                                h.tensor_tensor(o, a[:], b2[:],
                                                                ALU.mult)),
                                        reads=[bsg, bpsu], writes=[bheb[hslot]])
                                kb.emit("dve", (lambda h, o=hsl, b2=cbq[tt]:
                                                h.tensor_tensor(o, o, b2[:],
                                                                ALU.mult)),
                                        reads=[bheb[hslot], bcbq[tt]],
                                        writes=[bheb[hslot]])
                    nic_p = ic1 - ic0
                    for dc in (range(NDC) if "down" in _FFN_PARTS else ()):
                        wd_s, bwd = wdpool[dc % 2], bwdpool[dc % 2]
                        src = (sd_t if shared else wd_t[e - 1])
                        ap = src[dc][:, ic0:ic1, :]
                        dma("gp" if shared else "sync",
                            wd_s[:, 0:nic_p * 128].rearrange(
                                "p (ic d) -> p ic d", ic=nic_p),
                            ap, writes=[bwd], key=f"wdp{dc % 2}")
                        for tt in range(2):
                            ps, bps = pb[4 + tt % 2], bpb[4 + tt % 2]
                            fns = []
                            for l in range(nic_p):
                                fns.append(lambda h, o=ps,
                                           w=wd_s[:, l * 128:(l + 1) * 128],
                                           r=heb[:, l * HALF + tt * TT:
                                                  l * HALF + (tt + 1) * TT],
                                           st=(l == 0), sp=(l == nic_p - 1):
                                           h.matmul(o[:], w, r, start=st, stop=sp))
                            kb.emit("te", fns,
                                    reads=[bwd] + [bheb[l] for l in range(nic_p)],
                                    writes=[bps])
                            dst = xs(dc, H * HALF + tt * TT)
                            kb.emit("dve", (lambda h, o=dst, i=ps:
                                            h.tensor_tensor(o, o, i[:], ALU.add)),
                                    reads=[bx[dc], bps], writes=[bx[dc]])
            kb.barrier()
            sc.release(m)

    lay_a = dict(wq=wqa, wk=wka, wv=wva, wo=woa, causal=True, rope=False,
                 bias=False)
    lay_b = dict(wq=wqb, wk=wkb, wv=wvb, wo=wob, causal=False, rope=True,
                 bias=True)

    kb.barrier()
    emit_attn(lay_a)
    if phases != "attn_a":
        emit_ffn()
    if phases in ("all", "attn_a+ffn+attn_b"):
        emit_attn(lay_b)
    if phases == "all":
        emit_ffn()

    for i in range(8):
        dma("sync",
            y_out[i * 2:(i + 1) * 2].rearrange("dc p t -> p dc t"),
            x_fm[:, i * 2 * S:(i + 1) * 2 * S].rearrange("p (dc t) -> p dc t",
                                                         dc=2),
            reads=[bx[dc] for dc in range(2 * i, 2 * i + 2)],
            key=f"y{i % 2}")
    kb._wait("sync", "DS_y0", kb.cnt["DS_y0"])
    kb._wait("sync", "DS_y1", kb.cnt["DS_y1"])

    kb.finalize()
    sc.release(0)
    for cm in reversed(kb.semcm):
        cm.__exit__(None, None, None)
    return nc


_prog_cache = {}


def kernel(**inputs):
    phases = _PHASES
    if phases not in _prog_cache:
        _prog_cache[phases] = build_program(phases)
    nc = _prog_cache[phases]
    shared = _host_shared(inputs)
    in_maps = []
    for b in range(8):
        m = dict(shared)
        m.update(_host_layouts(inputs, b))
        in_maps.append(m)
    res = run_bass_kernel_spmd(nc, in_maps, core_ids=list(range(8)))
    outs = []
    for b in range(8):
        y = np.asarray(res.results[b]["y"])
        outs.append(y.reshape(D, S).T)
    return np.stack(outs).astype(np.float32)
